# version 10
# baseline (speedup 1.0000x reference)
import numpy as np
import ml_dtypes
import concourse.bass as bass
import concourse.mybir as mybir
from concourse.bass_utils import run_bass_kernel_spmd

F32 = mybir.dt.float32
BF16 = mybir.dt.bfloat16
AF = mybir.ActivationFunctionType
ALU = mybir.AluOpType
AX = mybir.AxisListType

B, S, D = 4, 4096, 1024
T = 2048
CC = 8
NE = 8
HC = 22
SHC = 12
NG = 4
NQB = 8
NKC = 32
NTP = 2
NTT = 2
EPS = 1e-5
KB = 1024

EXPS = list(range(NE)) + ["sh"]


def _hc(e):
    return SHC if e == "sh" else HC


AC_UNITS = sum(_hc(e) for e in EXPS) * NTP
Y_UNITS = len(EXPS) * CC * NTP
TZ_UNITS = len(EXPS) * CC * NTP * 2
MOE_Q = NE * CC * NTP * 2
W13_SLABS = sum(_hc(e) // 2 for e in EXPS) * NTP
W2_SLABS = len(EXPS) * CC * NTP
LN_TILES = 4
KCS = [4 * (qb + 1) for qb in range(NQB)]
SC_TILES = 2 * sum(KCS)
SC_PAIRS = SC_TILES // 2
AO_UNITS = 2 * NQB


class C:
    pass


def build():
    nc = bass.Bass("TRN2", target_bir_lowering=False, num_devices=8)

    xT = nc.dram_tensor("xT", [D, T], F32, kind="ExternalInput").ap()
    w1 = nc.dram_tensor("w1", [NE, D, HC * 128], F32, kind="ExternalInput").ap()
    w2 = nc.dram_tensor("w2", [NE, HC * 128, D], F32, kind="ExternalInput").ap()
    w3 = nc.dram_tensor("w3", [NE, D, HC * 128], F32, kind="ExternalInput").ap()
    sw1 = nc.dram_tensor("sw1", [D, SHC * 128], F32, kind="ExternalInput").ap()
    sw2 = nc.dram_tensor("sw2", [SHC * 128, D], F32, kind="ExternalInput").ap()
    sw3 = nc.dram_tensor("sw3", [D, SHC * 128], F32, kind="ExternalInput").ap()
    gwT = nc.dram_tensor("gwT", [D, NE], F32, kind="ExternalInput").ap()
    wqT = nc.dram_tensor("wqT", [D, 512], F32, kind="ExternalInput").ap()
    wkT = nc.dram_tensor("wkT", [D, 512], F32, kind="ExternalInput").ap()
    wvT = nc.dram_tensor("wvT", [D, 512], F32, kind="ExternalInput").ap()
    wpT = nc.dram_tensor("wpT", [D, D], F32, kind="ExternalInput").ap()
    bp = nc.dram_tensor("bp", [D], F32, kind="ExternalInput").ap()
    yT = nc.dram_tensor("yT", [D, T], F32, kind="ExternalOutput").ap()

    cc1_in = nc.dram_tensor("cc1_in", [D, T], BF16)
    cc1_out = nc.dram_tensor("cc1_out", [2, D, T], BF16)
    cc2_in = nc.dram_tensor("cc2_in", [512, S], BF16)
    cc2_out = nc.dram_tensor("cc2_out", [2, 512, S], BF16)

    mask_np = np.zeros((4, 128, 512), dtype=ml_dtypes.bfloat16)
    bigv = 3.3895314e38
    for a in range(4):
        kk = np.arange(128)[:, None]
        qq = np.arange(512)[None, :]
        mask_np[a] = ((kk <= qq - 128 * a) * bigv).astype(ml_dtypes.bfloat16)
    masks_d = nc.inline_tensor(np.ascontiguousarray(mask_np.transpose(1, 0, 2)),
                               name="cmasks")
    ones_d = nc.inline_tensor(np.ones((1, 128), np.float32), name="cones")
    onesc_d = nc.inline_tensor(np.ones((128, 1), dtype=ml_dtypes.bfloat16), name="conesc")
    idf_d = nc.inline_tensor(np.eye(128, dtype=np.float32), name="cidf")
    eps64_np = np.zeros((1, 65), dtype=ml_dtypes.bfloat16); eps64_np[0, 64] = 1.0
    eps64_d = nc.inline_tensor(eps64_np, name="ceps64")
    ones512_d = nc.inline_tensor(np.full((1, 512), 1e-9, dtype=ml_dtypes.bfloat16),
                                 name="cones512")
    selm_np = np.zeros((NE, NE * 128), np.float32)
    for e in range(NE):
        selm_np[e, e * 128:(e + 1) * 128] = 1.0
    selm_d = nc.inline_tensor(selm_np, name="cselm")
    idb_d = nc.inline_tensor(np.eye(128, dtype=ml_dtypes.bfloat16), name="cidb")

    def at(name, shape, dtype, off):
        return nc.alloc_sbuf_tensor_at(name, shape, dtype, offset=off)

    MB = 17920
    xF = at("xF", [128, CC, T], F32, MB)
    A0 = MB + 64 * KB
    xn = at("xn", [128, CC, T], BF16, A0)
    h = at("h", [128, HC, 1024], BF16, A0 + 32 * KB)
    w1s = at("w1s", [128, 2, CC, 2, 128], BF16, A0 + 76 * KB)
    w3s = at("w3s", [128, 2, CC, 2, 128], BF16, A0 + 84 * KB)
    w2s = at("w2s", [128, 2, HC, 128], BF16, A0 + 92 * KB)
    cbt = at("cbt", [128, T], F32, A0 + 105 * KB + 512)
    lg = at("lg", [8, T], F32, A0 + 105 * KB + 512)
    tq = at("tq", [128, 2, 512], F32, A0 + 113 * KB + 512)
    combT = at("combT", [NE, T], F32, A0 + 117 * KB + 512)
    xb = at("xb", [128, 2, 512], BF16, A0 + 125 * KB + 512)
    sqb = at("sqb", [128, 2, 512], BF16, A0 + 127 * KB + 512)
    gt = at("gt", [128, 16, NE], F32, A0 + 129 * KB + 512)
    gs = at("gs", [128, 16, 4], F32, A0 + 130 * KB)
    gmsk = at("gmsk", [128, 16, NE], F32, A0 + 130 * KB + 256)
    selm_sb = at("selm_sb", [NE, NE * 128], F32, A0 + 130 * KB + 768)
    FFN_HIGH = A0 + 134 * KB + 768

    xn1 = at("xn1", [128, CC, S], BF16, A0)
    AT = A0 + 64 * KB
    Kg = at("Kg", [128, S], BF16, AT)
    Vg = at("Vg", [128, NKC, 130], BF16, AT + 8 * KB)
    Qg = at("Qg", [128, S], BF16, AT + 16 * KB + 512)
    Vf = at("Vf", [128, 2, 512], BF16, AT + 24 * KB + 512)
    ao = at("ao", [128, S], BF16, AT + 26 * KB + 512)
    wq_sb = at("wq_sb", [128, CC, 128], BF16, AT + 34 * KB + 512)
    wk_sb = at("wk_sb", [128, CC, 128], BF16, AT + 36 * KB + 512)
    wv_sb = at("wv_sb", [128, CC, 128], BF16, AT + 38 * KB + 512)
    e_scr = at("e_scr", [128, 2, 1024], BF16, AT + 40 * KB + 512)
    eps64_sb = at("eps64_sb", [1, 65], BF16, AT + 44 * KB + 512)
    ones512 = at("ones512", [1, 512], BF16, AT + 44 * KB + 512 + 160)
    sp = at("sp", [128, 4, 512], BF16, AT + 48 * KB + 512)
    pp = at("pp", [128, 4, 512], BF16, AT + 52 * KB + 512)
    msp = at("msp", [128, 4, 512], BF16, AT + 56 * KB + 512)
    rbv = at("rbv", [128, 2, 512], BF16, AT + 60 * KB + 512)
    masks = at("cm_sb", [128, 4, 512], BF16, AT + 62 * KB + 512)
    idb_sb = at("idb_sb", [128, 128], BF16, AT + 66 * KB + 512)
    ATT_HIGH = AT + 66 * KB + 768
    assert ATT_HIGH <= A0 + 130 * KB + 768, (ATT_HIGH, A0)
    ao2 = at("ao2", [128, CC, T], BF16, A0)
    wp_sb = at("wp_sb", [128, CC, 1024], BF16, A0 + 32 * KB)

    ST = FFN_HIGH
    ones_row = at("ones_row", [1, 128], F32, ST)
    ones_col = at("ones_col", [128, 1], BF16, ST + 512)
    b_sb = at("b_sb", [128, CC], F32, ST + 544)
    gw_sb = at("gw_sb", [128, CC, NE], BF16, ST + 576)
    idf_sb = at("idf_sb", [128, 128], F32, ST + 704)
    st_m = at("st_m", [1, 512], F32, ST + 1216)
    st_r = at("st_r", [1, 512], F32, ST + 3264)
    st_t = at("st_t", [1, 512], F32, ST + 5312)
    assert ST + 7360 <= 229376, (ST,)

    psA = nc.alloc_psum_tensor("psA", [128, 1024], F32)
    psC = nc.alloc_psum_tensor("psC", [128, 1024], F32)
    psY = nc.alloc_psum_tensor("psY", [128, 1024], F32)
    psS = nc.alloc_psum_tensor("psS", [128, 512], F32)
    pt = nc.alloc_psum_tensor("pt", [128, 512], BF16)

    sem = nc.alloc_semaphore
    s_xd = sem("s_xd")
    s_gw = sem("s_gw")
    s_w13 = [sem("s_w13_0"), sem("s_w13_1")]
    s_w2 = [sem("s_w2_0"), sem("s_w2_1")]
    s_te, s_ta, s_tv, s_ty, s_tz, s_tq = (sem("s_te"), sem("s_ta"), sem("s_tv"),
                                          sem("s_ty"), sem("s_tz"), sem("s_tq"))
    s_ln, s_lnm, s_lne, s_lnb, s_lnx = (sem("s_ln"), sem("s_lnm"), sem("s_lne"),
                                        sem("s_lnb"), sem("s_lnx"))
    s_lnc, s_lnv, s_lns, s_lnq = sem("s_lnc"), sem("s_lnv"), sem("s_lns"), sem("s_lnq")
    s_lnp = sem("s_lnp")
    s_gl, s_glv, s_gtr, s_gte = sem("s_gl"), sem("s_glv"), sem("s_gtr"), sem("s_gte")
    s_gop, s_gx = sem("s_gop"), sem("s_gx")
    s_cbm, s_cbv = sem("s_cbm"), sem("s_cbv")
    s_cc1, s_cc2 = sem("s_cc1"), sem("s_cc2")
    s_x1d, s_x1r = sem("s_x1d"), sem("s_x1r")
    s_kv, s_kve, s_vt, s_vte, s_aw = (sem("s_kv"), sem("s_kve"), sem("s_vt"),
                                      sem("s_vte"), sem("s_aw"))
    s_tes, s_tae, s_tal, s_tvp, s_msp = (sem("s_tes"), sem("s_tae"), sem("s_tal"),
                                         sem("s_tvp"), sem("s_msp"))
    s_tpv, s_den, s_tve = sem("s_tpv"), sem("s_den"), sem("s_tve")
    s_pr, s_prv, s_aod, s_pw, s_ao2 = (sem("s_pr"), sem("s_prv"), sem("s_aod"),
                                       sem("s_pw"), sem("s_ao2"))
    s_st = sem("s_st")
    s_ph = [sem(f"s_ph{i}") for i in range(8)]

    slab13_consumed = {}
    slab2_consumed = {}
    _te = 0
    _ty = 0
    _i13 = 0
    _i2 = 0
    for call in range(2):
        for e in EXPS:
            hcn = _hc(e)
            for tp in range(NTP):
                for hcp in range(hcn // 2):
                    slab13_consumed[_i13] = _te + (2 * hcp + 2)
                    _i13 += 1
                _te += hcn
                for fc in range(CC):
                    slab2_consumed[_i2] = _ty + 1
                    _ty += 1
                    _i2 += 1

    def fb(call):
        return dict(te=call * AC_UNITS, ty=call * Y_UNITS, tz=call * TZ_UNITS,
                    tq=call * MOE_Q, i13=call * W13_SLABS, i2=call * W2_SLABS,
                    cbm=call * NE * 4)

    def lnb(k):
        return dict(ln=k * LN_TILES * CC, lnm=k * LN_TILES, lne=k * LN_TILES,
                    lnb=k * LN_TILES, lnx=k * LN_TILES * CC, lnc=k * LN_TILES * CC,
                    lnv=k * LN_TILES, lns=k * LN_TILES, lnq=k * LN_TILES * CC,
                    lnp=k * LN_TILES * CC)

    def gb(call):
        return dict(gl=call * 4, glv=call * 4, gtr=call * 32, gte=call * 32,
                    gop=call * 11 * 16, gx=call * 16)

    def barrier(i, eng, is_dma=False, waits=()):
        for (ss, vv) in waits:
            eng.wait_ge(ss, vv)
        if not is_dma:
            eng.drain()
        eng.sem_inc(s_ph[i], 1)
        eng.wait_ge(s_ph[i], 5)

    def ln_tensor(t, k):
        b = lnb(k)
        if k == 1:
            t.wait_ge(s_tz, TZ_UNITS)
            t.wait_ge(s_cbv, 32)
        for i in range(LN_TILES):
            for cc in range(CC):
                t.wait_ge(s_ln, b["ln"] + i * CC + cc + 1)
                t.matmul(psS[0:1, :], ones_col[:, 0:1], xb[:, cc % 2, :],
                         start=(cc == 0), stop=(cc == CC - 1))
                t.matmul(psY[0:1, 0:512], ones_col[:, 0:1], sqb[:, cc % 2, :],
                         start=(cc == 0), stop=(cc == CC - 1)).then_inc(s_lnp, 1)
            t.wait_ge(s_lne, b["lne"] + i + 1)
            t.matmul(psA[:, 0:512], ones_row[0:1, :], st_r[0:1, :], start=True, stop=True)
            t.matmul(psC[:, 0:512], ones_row[0:1, :], st_m[0:1, :],
                     start=True, stop=True).then_inc(s_lnb, 1)
            if i < LN_TILES - 1:
                t.wait_ge(s_lnx, b["lnx"] + (i + 1) * CC)

    def ln_scalar(s, k):
        b = lnb(k)
        for i in range(LN_TILES):
            s.wait_ge(s_lnv, b["lnv"] + i + 1)
            s.activation(st_r[0:1, :], st_t[0:1, :], AF.Sqrt).then_inc(s_lns, 1)

    def ln_vector(v, k):
        b = lnb(k)
        for i in range(LN_TILES):
            tok = i * 512
            for cc in range(CC + 1):
                if cc < CC:
                    idx = i * CC + cc
                    if idx >= 2:
                        v.wait_ge(s_lnp, b["lnp"] + idx - 1)
                    v.tensor_copy(xb[:, cc % 2, :], xF[:, cc, tok:tok + 512]).then_inc(s_lnc, 1)
                if cc >= 1:
                    ccp = cc - 1
                    v.wait_ge(s_lnc, b["lnc"] + i * CC + ccp + 1)
                    v.tensor_tensor(sqb[:, ccp % 2, :], xb[:, ccp % 2, :],
                                    xb[:, ccp % 2, :], ALU.mult).then_inc(s_ln, 1)
            v.wait_ge(s_lnp, b["lnp"] + (i + 1) * CC)
            v.wait_ge(s_lns, b["lns"] + i)
            v.tensor_scalar(st_m[0:1, :], psS[0:1, :], 1.0 / D, None, ALU.mult)
            v.tensor_scalar(st_t[0:1, :], psY[0:1, 0:512], 1.0 / D, None, ALU.mult)
            v.drain()
            v.tensor_tensor(st_r[0:1, :], st_m[0:1, :], st_m[0:1, :], ALU.mult)
            v.drain()
            v.tensor_tensor(st_t[0:1, :], st_t[0:1, :], st_r[0:1, :], ALU.subtract)
            v.drain()
            v.tensor_scalar(st_t[0:1, :], st_t[0:1, :], EPS, None,
                            ALU.add).then_inc(s_lnv, 1)
            v.wait_ge(s_lns, b["lns"] + i + 1)
            v.reciprocal(st_r[0:1, :], st_r[0:1, :])
            v.drain()
            v.tensor_tensor(st_m[0:1, :], st_m[0:1, :], st_r[0:1, :],
                            ALU.mult).then_inc(s_lne, 1)
            v.wait_ge(s_lnb, b["lnb"] + i + 1)
            for cc in range(CC + 1):
                if cc < CC:
                    if i * CC + cc >= 2:
                        v.wait_ge(s_lnx, b["lnx"] + i * CC + cc - 1)
                    v.tensor_tensor(tq[:, cc % 2, :], xF[:, cc, tok:tok + 512],
                                    psA[:, 0:512], ALU.mult).then_inc(s_lnq, 1)
                if cc >= 1:
                    ccp = cc - 1
                    v.wait_ge(s_lnq, b["lnq"] + i * CC + ccp + 1)
                    v.tensor_tensor(xn[:, ccp, tok:tok + 512], tq[:, ccp % 2, :],
                                    psC[:, 0:512], ALU.subtract).then_inc(s_lnx, 1)

    def gate_tensor(t, call):
        b = gb(call)
        bl = lnb(2 * call)
        t.wait_ge(s_lnx, bl["lnx"] + LN_TILES * CC)
        for i in range(4):
            if i > 0:
                t.wait_ge(s_glv, b["glv"] + i)
            for cc in range(CC):
                ins = t.matmul(psS[0:NE, :], gw_sb[:, cc, :],
                               xn[:, cc, i * 512:(i + 1) * 512],
                               start=(cc == 0), stop=(cc == CC - 1))
            ins.then_inc(s_gl, 1)
        t.wait_ge(s_glv, b["glv"] + 4)
        for j in range(16):
            if j > 0:
                t.wait_ge(s_gte, b["gte"] + j)
            t.transpose(psS[:, 0:NE], lg[0:NE, j * 128:(j + 1) * 128],
                        idf_sb[0:NE, 0:NE]).then_inc(s_gtr, 1)
        t.wait_ge(s_gop, b["gop"] + 11 * 16)
        for j in range(16):
            if j > 0:
                t.wait_ge(s_gte, b["gte"] + 16 + j)
            t.transpose(psS[0:NE, 0:128], gt[:, j, :],
                        idf_sb[:, :]).then_inc(s_gtr, 1)

    def gate_scalar(s, call):
        b = gb(call)
        for j in range(16):
            s.wait_ge(s_gop, b["gop"] + j + 1)
            s.activation(gt[:, j, :], gt[:, j, :], AF.Exp,
                         bias=gs[:, j, 0:1]).then_inc(s_gx, 1)

    def gate_vector(v, call):
        b = gb(call)
        for i in range(4):
            v.wait_ge(s_gl, b["gl"] + i + 1)
            v.tensor_copy(lg[0:NE, i * 512:(i + 1) * 512], psS[0:NE, :]).then_inc(s_glv, 1)
        for j in range(16):
            v.wait_ge(s_gtr, b["gtr"] + j + 1)
            v.tensor_copy(gt[:, j, :], psS[:, 0:NE]).then_inc(s_gte, 1)
        gop = b["gop"]

        for j in range(16):
            v.wait_ge(s_gte, b["gte"] + j + 1)
            v.reduce_max(gs[:, j, 0:1], gt[:, j, :], axis=AX.X,
                         negate=True).then_inc(s_gop, 1)
        gop += 16
        for j in range(16):
            v.wait_ge(s_gx, b["gx"] + j + 1)
            v.reduce_sum(gs[:, j, 1:2], gt[:, j, :], axis=AX.X).then_inc(s_gop, 1)
        gop += 16

        def step(fn):
            nonlocal gop
            for j in range(16):
                v.wait_ge(s_gop, gop - 16 + j + 1)
                fn(j).then_inc(s_gop, 1)
            gop += 16

        step(lambda j: v.reciprocal(gs[:, j, 2:3], gs[:, j, 1:2]))
        step(lambda j: v.tensor_scalar(gt[:, j, :], gt[:, j, :], gs[:, j, 2:3],
                                       None, ALU.mult))
        step(lambda j: v.reduce_max(gs[:, j, 3:4], gt[:, j, :], axis=AX.X))
        step(lambda j: v.tensor_scalar(gmsk[:, j, :], gt[:, j, :], gs[:, j, 3:4],
                                       None, ALU.is_lt))
        step(lambda j: v.tensor_tensor(gmsk[:, j, :], gmsk[:, j, :], gt[:, j, :],
                                       ALU.mult))
        step(lambda j: v.reduce_max(gs[:, j, 3:4], gmsk[:, j, :], axis=AX.X))
        step(lambda j: v.tensor_scalar(gmsk[:, j, :], gt[:, j, :], gs[:, j, 3:4],
                                       None, ALU.is_ge))
        step(lambda j: v.tensor_tensor(gt[:, j, :], gt[:, j, :], gmsk[:, j, :],
                                       ALU.mult))
        step(lambda j: v.tensor_scalar(gt[:, j, :], gt[:, j, :], 0.5, None, ALU.mult))
        for j in range(16):
            v.wait_ge(s_gtr, b["gtr"] + 16 + j + 1)
            v.tensor_copy(combT[0:NE, j * 128:(j + 1) * 128],
                          psS[0:NE, 0:128]).then_inc(s_gte, 1)

    def ffn_gpsimd(g, call, lim13=None, lim2=None, skip13=0, skip2=0):
        b = fb(call)
        i13 = b["i13"]
        i2 = b["i2"]
        n13 = 0
        n2 = 0
        for e in EXPS:
            hcn = _hc(e)
            for tp in range(NTP):
                for hcp in range(hcn // 2):
                    n13 += 1
                    if (lim13 is not None and n13 > lim13) or n13 <= skip13:
                        i13 += 1
                        continue
                    buf = i13 % 2
                    if i13 >= 2:
                        g.wait_ge(s_te, slab13_consumed[i13 - 2])
                    if e == "sh":
                        sA = sw1.rearrange("(cc p) hh -> p cc hh", p=128)
                        sB = sw3.rearrange("(cc p) hh -> p cc hh", p=128)
                    else:
                        sA = w1[e].rearrange("(cc p) hh -> p cc hh", p=128)
                        sB = w3[e].rearrange("(cc p) hh -> p cc hh", p=128)
                    g.dma_start(out=w1s[:, buf, :, :, :].rearrange("p cc x w -> p cc (x w)"),
                                in_=sA[:, :, hcp * 256:(hcp + 1) * 256]).then_inc(s_w13[buf], 16)
                    g.dma_start(out=w3s[:, buf, :, :, :].rearrange("p cc x w -> p cc (x w)"),
                                in_=sB[:, :, hcp * 256:(hcp + 1) * 256]).then_inc(s_w13[buf], 16)
                    i13 += 1
                for fc in range(CC):
                    n2 += 1
                    if (lim2 is not None and n2 > lim2) or n2 <= skip2:
                        i2 += 1
                        continue
                    buf = i2 % 2
                    if i2 >= 2:
                        g.wait_ge(s_ty, slab2_consumed[i2 - 2])
                    if e == "sh":
                        src2 = sw2.rearrange("(hc p) f -> p hc f", p=128)[
                            :, :, fc * 128:(fc + 1) * 128]
                        dst2 = w2s[:, buf, 0:SHC, :]
                    else:
                        src2 = w2[e].rearrange("(hc p) f -> p hc f", p=128)[
                            :, :, fc * 128:(fc + 1) * 128]
                        dst2 = w2s[:, buf, :, :]
                    g.dma_start(out=dst2, in_=src2).then_inc(s_w2[buf], 16)
                    i2 += 1

    def ffn_tensor(t, call):
        b = fb(call)
        uid = b["te"]
        yid = b["ty"]
        tzc = b["tz"]
        i13 = b["i13"]
        i2 = b["i2"]
        cbm = b["cbm"]
        for ei, e in enumerate(EXPS):
            hcn = _hc(e)
            for tp in range(NTP):
                tok = tp * 1024
                t.wait_ge(s_tz, tzc)
                for hc in range(hcn):
                    hcp, hcx = divmod(hc, 2)
                    slab = i13 + hcp
                    buf = slab % 2
                    if hcx == 0:
                        t.wait_ge(s_w13[buf], (slab // 2 + 1) * 32)
                    ab = psA if uid % 2 == 0 else psY
                    if uid >= 2:
                        t.wait_ge(s_ta, uid - 1)
                    for cc in range(CC):
                        t.matmul(ab[:, 0:512], w1s[:, buf, cc, hcx, :],
                                 xn[:, cc, tok:tok + 512],
                                 start=(cc == 0), stop=(cc == CC - 1))
                        t.matmul(ab[:, 512:1024], w1s[:, buf, cc, hcx, :],
                                 xn[:, cc, tok + 512:tok + 1024],
                                 start=(cc == 0), stop=(cc == CC - 1))
                    if uid >= 1:
                        t.wait_ge(s_tv, uid)
                    for cc in range(CC):
                        t.matmul(psC[:, 0:512], w3s[:, buf, cc, hcx, :],
                                 xn[:, cc, tok:tok + 512],
                                 start=(cc == 0), stop=(cc == CC - 1))
                        ins = t.matmul(psC[:, 512:1024], w3s[:, buf, cc, hcx, :],
                                       xn[:, cc, tok + 512:tok + 1024],
                                       start=(cc == 0), stop=(cc == CC - 1))
                    ins.then_inc(s_te, 1)
                    uid += 1
                i13 += hcn // 2
                if tp == 0 and e != "sh":
                    t.wait_ge(s_gte, gb(call)["gte"] + 32)
                    for j in range(4):
                        if cbm > 0:
                            t.wait_ge(s_cbv, cbm)
                        t.matmul(psS[:, :], selm_sb[:, e * 128:(e + 1) * 128],
                                 combT[0:NE, j * 512:(j + 1) * 512],
                                 start=True, stop=True).then_inc(s_cbm, 1)
                        cbm += 1
                t.wait_ge(s_tv, uid)
                for fc in range(CC):
                    slab2 = i2 + fc
                    buf = slab2 % 2
                    t.wait_ge(s_w2[buf], (slab2 // 2 + 1) * 16)
                    yb = psY if (yid % 2 == 0) else psA
                    if yid >= 2:
                        t.wait_ge(s_tz, tzc - 2)
                    for hc in range(hcn):
                        t.matmul(yb[:, 0:512], w2s[:, buf, hc, :],
                                 h[:, hc, 0:512],
                                 start=(hc == 0), stop=(hc == hcn - 1))
                        ins = t.matmul(yb[:, 512:1024], w2s[:, buf, hc, :],
                                       h[:, hc, 512:1024],
                                       start=(hc == 0), stop=(hc == hcn - 1))
                    ins.then_inc(s_ty, 1)
                    yid += 1
                    tzc += 2
                i2 += CC

    def ffn_scalar(s, call):
        b = fb(call)
        uid = b["te"]
        yid = b["ty"]
        for ei, e in enumerate(EXPS):
            hcn = _hc(e)
            for tp in range(NTP):
                if ei > 0 or tp > 0:
                    s.wait_ge(s_ty, yid)
                yid += CC
                for hc in range(hcn):
                    s.wait_ge(s_te, uid + 1)
                    ab = psA if uid % 2 == 0 else psY
                    s.activation(h[:, hc, :], ab[:, :], AF.Silu).then_inc(s_ta, 1)
                    uid += 1

    def ffn_vector(v, call):
        b = fb(call)
        uid = b["te"]
        yid = b["ty"]
        qid = b["tq"]
        tzc = b["tz"]
        cbv = b["cbm"]
        pend = None
        tz_of_add = {}

        def flush():
            nonlocal pend, tzc
            if pend is not None:
                v.wait_ge(s_tq, pend[0] + 1)
                v.tensor_tensor(pend[1], pend[1], tq[:, pend[0] % 2, :],
                                ALU.add).then_inc(s_tz, 1)
                tzc += 1
                tz_of_add[pend[0]] = tzc
                pend = None

        for ei, e in enumerate(EXPS):
            hcn = _hc(e)
            for tp in range(NTP):
                tok = tp * 1024
                flush()
                if ei > 0 or tp > 0:
                    v.wait_ge(s_ty, yid)
                for hc in range(hcn):
                    v.wait_ge(s_ta, uid + 1)
                    v.tensor_tensor(h[:, hc, :], h[:, hc, :], psC[:, :],
                                    ALU.mult).then_inc(s_tv, 1)
                    uid += 1
                if tp == 0 and e != "sh":
                    flush()
                    v.wait_ge(s_tz, tzc)
                    for j in range(4):
                        v.wait_ge(s_cbm, cbv + j + 1)
                        v.tensor_copy(cbt[:, j * 512:(j + 1) * 512], psS[:, :]).then_inc(s_cbv, 1)
                    cbv += 4
                for fc in range(CC):
                    yb = psY if (yid % 2 == 0) else psA
                    v.wait_ge(s_ty, yid + 1)
                    for half in range(2):
                        tk = tok + half * 512
                        if e == "sh":
                            flush()
                            v.scalar_tensor_tensor(xF[:, fc, tk:tk + 512],
                                                   yb[:, half * 512:(half + 1) * 512], 0.5,
                                                   xF[:, fc, tk:tk + 512],
                                                   ALU.mult, ALU.add).then_inc(s_tz, 1)
                            tzc += 1
                        else:
                            if qid >= 2 and (qid - 2) in tz_of_add:
                                v.wait_ge(s_tz, tz_of_add[qid - 2])
                            v.tensor_tensor(tq[:, qid % 2, :],
                                            yb[:, half * 512:(half + 1) * 512],
                                            cbt[:, tk:tk + 512], ALU.mult).then_inc(s_tq, 1)
                            flush()
                            pend = (qid, xF[:, fc, tk:tk + 512])
                            qid += 1
                    yid += 1
        flush()

    def att_gpsimd(g, grp):
        if grp >= 1:
            g.wait_ge(s_kv, 24 * grp)
        for w_in, w_out in ((wqT, wq_sb), (wkT, wk_sb), (wvT, wv_sb)):
            g.dma_start(out=w_out[:, :, :],
                        in_=w_in.rearrange("(cc p) f -> p cc f", p=128)[
                            :, :, grp * 128:(grp + 1) * 128]).then_inc(s_aw, 16)

    def att_tensor_kvq(t, grp):
        t.wait_ge(s_aw, (grp + 1) * 48)
        if grp >= 1:
            t.wait_ge(s_tae, grp * SC_PAIRS)
        slots = [psA[:, 0:512], psA[:, 512:1024], psC[:, 0:512], psC[:, 512:1024]]
        kv = grp * 24
        vt = grp * 32
        for tt in range(8):
            if kv >= 4:
                t.wait_ge(s_kve, kv - 3)
            for cc in range(CC):
                ins = t.matmul(slots[kv % 4], wk_sb[:, cc, :],
                               xn1[:, cc, tt * 512:(tt + 1) * 512],
                               start=(cc == 0), stop=(cc == CC - 1))
            ins.then_inc(s_kv, 1)
            kv += 1
        for tt in range(8):
            if kv >= 4:
                t.wait_ge(s_kve, kv - 3)
            for cc in range(CC):
                ins = t.matmul(slots[kv % 4], wv_sb[:, cc, :],
                               xn1[:, cc, tt * 512:(tt + 1) * 512],
                               start=(cc == 0), stop=(cc == CC - 1))
            ins.then_inc(s_kv, 1)
            kv += 1
            for j in range(4):
                t.wait_ge(s_kve, kv)
                if vt >= 1:
                    t.wait_ge(s_vte, vt)
                t.transpose(pt[:, 0:128],
                            Vf[:, tt % 2, j * 128:(j + 1) * 128],
                            idb_sb[:, :]).then_inc(s_vt, 1)
                vt += 1
        for tt in range(8):
            if kv >= 4:
                t.wait_ge(s_kve, kv - 3)
            for cc in range(CC):
                ins = t.matmul(slots[kv % 4], wq_sb[:, cc, :],
                               xn1[:, cc, tt * 512:(tt + 1) * 512],
                               start=(cc == 0), stop=(cc == CC - 1))
            ins.then_inc(s_kv, 1)
            kv += 1

    def att_vector_kvq(v, grp):
        slots = [psA[:, 0:512], psA[:, 512:1024], psC[:, 0:512], psC[:, 512:1024]]
        kv = grp * 24
        vt = grp * 32
        if grp >= 1:
            v.wait_ge(s_tpv, grp * AO_UNITS)
        v.memset(Vg.ap().rearrange("p k (x f) -> p k x f", x=2)[:, :, :, 64:65], 1.0)
        for tt in range(8):
            v.wait_ge(s_kv, kv + 1)
            v.tensor_copy(Kg[:, tt * 512:(tt + 1) * 512], slots[kv % 4]).then_inc(s_kve, 1)
            kv += 1
        for tt in range(8):
            v.wait_ge(s_kv, kv + 1)
            v.tensor_copy(Vf[:, tt % 2, :], slots[kv % 4]).then_inc(s_kve, 1)
            kv += 1
            for j in range(4):
                kc = tt * 4 + j
                v.wait_ge(s_vt, vt + 1)
                v.tensor_copy(
                    Vg[:, kc, :].rearrange("p (x f) -> p x f", x=2)[:, :, 0:64],
                    pt[:, 0:128].rearrange("p (x f) -> p x f", x=2)[:, :, 0:64],
                ).then_inc(s_vte, 1)
                vt += 1
        for tt in range(8):
            v.wait_ge(s_kv, kv + 1)
            v.tensor_copy(Qg[:, tt * 512:(tt + 1) * 512], slots[kv % 4]).then_inc(s_kve, 1)
            kv += 1

    def att_pairs():
        pairs = []
        for qb in range(NQB):
            L = KCS[qb]
            for pu in range(L // 2):
                for h2 in range(2):
                    pairs.append((h2, qb, 2 * pu, pu == 0, pu == L // 2 - 1,
                                  2 * qb + h2))
        return pairs

    def att_tensor_scores(t, grp):
        pj0 = grp * SC_PAIRS
        gu0 = grp * AO_UNITS
        t.wait_ge(s_kve, (grp + 1) * 24)
        t.wait_ge(s_vte, (grp + 1) * 32)
        pairs = att_pairs()
        pbank = [psA, psC]

        def emit_score_pair(j):
            h2, qb, kc, _, _, _ = pairs[j]
            gpj = pj0 + j
            if gpj >= 2:
                t.wait_ge(s_tae, gpj - 1)
            bk = pbank[gpj % 2]
            for half in range(2):
                ins = t.matmul(bk[:, half * 512:(half + 1) * 512],
                               Kg[h2 * 64:(h2 + 1) * 64,
                                  (kc + half) * 128:(kc + half + 1) * 128],
                               Qg[h2 * 64:(h2 + 1) * 64, qb * 512:(qb + 1) * 512],
                               start=True, stop=True)
            ins.then_inc(s_tes, 1)

        emit_score_pair(0)
        for j, (h2, qb, kc, u_start, u_end, lu) in enumerate(pairs):
            gpj = pj0 + j
            gu = gu0 + lu
            yh = psY[:, h2 * 512:(h2 + 1) * 512]
            if u_start:
                if gu >= 2:
                    t.wait_ge(s_tve, gu - 1)
                t.matmul(yh[0:65, :], eps64_sb[0:1, :], ones512[0:1, :],
                         start=True, stop=False)
            if j + 1 < len(pairs):
                emit_score_pair(j + 1)
            t.wait_ge(s_tvp, gpj + 1)
            for half in range(2):
                ins = t.matmul(yh[0:65, :],
                               Vg[:, kc + half, h2 * 65:(h2 + 1) * 65],
                               pp[:, (2 * gpj + half) % 4, :],
                               start=False, stop=(u_end and half == 1))
            if u_end:
                ins.then_inc(s_tpv, 1)
                t.wait_ge(s_den, gu + 1)
                if gu >= 1:
                    t.wait_ge(s_tve, gu)
                t.matmul(psS[0:64, :], ones_row[0:1, 0:64], st_r[0:1, :],
                         start=True, stop=True).then_inc(s_pr, 1)

    def att_scalar_scores(s, grp):
        pj0 = grp * SC_PAIRS
        pbank = [psA, psC]
        for j in range(SC_PAIRS):
            gpj = pj0 + j
            s.wait_ge(s_tes, gpj + 1)
            if gpj >= 2:
                s.wait_ge(s_tal, gpj - 1)
            s.activation(e_scr[:, gpj % 2, :], pbank[gpj % 2][:, :], AF.Exp,
                         scale=0.125).then_inc(s_tae, 1)
            s.wait_ge(s_tae, gpj + 1)
            if gpj >= 2:
                s.wait_ge(s_tvp, gpj - 1)
            s.activation(sp[:, (2 * gpj) % 4:(2 * gpj) % 4 + 2, :]
                         .rearrange("p a b -> p (a b)"),
                         e_scr[:, gpj % 2, :],
                         AF.Ln, bias=1.0).then_inc(s_tal, 1)

    def att_vector_scores(v, grp):
        pj0 = grp * SC_PAIRS
        gu0 = grp * AO_UNITS
        mc = grp * AO_UNITS * 2
        if grp >= 1:
            v.wait_ge(s_aod, grp * 16)
        pairs = att_pairs()
        np_us = {}
        for h2, qb, kc, u_start, u_end, lu in pairs:
            np_us[lu] = KCS[qb] // 2
        mspc = {}
        for j, (h2, qb, kc, u_start, u_end, lu) in enumerate(pairs):
            gpj = pj0 + j
            gu = gu0 + lu
            np_u = np_us[lu]
            pu = kc // 2
            sl = (2 * gpj) % 4
            v.wait_ge(s_tal, gpj + 1)
            if pu >= np_u - 2:
                ms = (mc % 2) * 2
                if gpj >= 2:
                    v.wait_ge(s_tvp, gpj - 1)
                v.tensor_tensor(msp[:, ms:ms + 2, :],
                                sp[:, sl:sl + 2, :],
                                masks[:, 2 * (pu - (np_u - 2)):2 * (pu - (np_u - 2)) + 2, :],
                                ALU.min).then_inc(s_msp, 1)
                mc += 1
                v.wait_ge(s_msp, mc)
                v.tensor_tensor(pp[:, sl:sl + 2, :],
                                msp[:, ms:ms + 2, :],
                                msp[:, ms:ms + 2, :],
                                ALU.mult).then_inc(s_tvp, 1)
            else:
                v.tensor_tensor(pp[:, sl:sl + 2, :].rearrange("p a b -> p (a b)"),
                                sp[:, sl:sl + 2, :].rearrange("p a b -> p (a b)"),
                                sp[:, sl:sl + 2, :].rearrange("p a b -> p (a b)"),
                                ALU.mult).then_inc(s_tvp, 1)
            if u_end:
                o2 = h2
                if gu >= 1:
                    v.wait_ge(s_pr, gu)
                v.wait_ge(s_tpv, gu + 1)
                v.reciprocal(st_r[0:1, :],
                             psY[64:65, o2 * 512:(o2 + 1) * 512]).then_inc(s_den, 1)
                v.wait_ge(s_pr, gu + 1)
                if gu >= 2:
                    v.wait_ge(s_tve, gu - 1)
                v.tensor_copy(rbv[0:64, o2, :], psS[0:64, :])
                v.drain()
                v.tensor_tensor(ao[h2 * 64:(h2 + 1) * 64, qb * 512:(qb + 1) * 512],
                                psY[0:64, o2 * 512:(o2 + 1) * 512], rbv[0:64, o2, :],
                                ALU.mult).then_inc(s_tve, 1)

    def att_sync(sy, grp):
        sy.wait_ge(s_tve, (grp + 1) * AO_UNITS)
        sy.dma_start(out=cc2_in.ap()[grp * 128:(grp + 1) * 128, :],
                     in_=ao[:, :]).then_inc(s_aod, 16)

    with nc.Block() as block:
        @block.gpsimd
        def _(g):
            g.dma_start(out=gw_sb[:, :, :],
                        in_=gwT.rearrange("(cc p) e -> p cc e", p=128)).then_inc(s_gw, 16)
            barrier(0, g, is_dma=True, waits=[(s_gw, 16)])
            ffn_gpsimd(g, 0)
            barrier(1, g, is_dma=True)
            g.wait_ge(s_x1d, 16)
            g.collective_compute("AllGather", ALU.bypass,
                                 replica_groups=[[0, 1], [2, 3], [4, 5], [6, 7]],
                                 ins=[cc1_in.ap()], outs=[cc1_out.ap()]).then_inc(s_cc1, 1)
            att_gpsimd(g, 0)
            barrier(2, g, is_dma=True)
            for grp in range(1, NG):
                att_gpsimd(g, grp)
            barrier(3, g, is_dma=True)
            g.dma_start(out=wp_sb[:, :, :],
                        in_=wpT.rearrange("(cc p) f -> p cc f", p=128)).then_inc(s_pw, 16)
            g.wait_ge(s_aod, NG * 16)
            g.collective_compute("AllGather", ALU.bypass,
                                 replica_groups=[[0, 1], [2, 3], [4, 5], [6, 7]],
                                 ins=[cc2_in.ap()], outs=[cc2_out.ap()]).then_inc(s_cc2, 1)
            ffn_gpsimd(g, 1, lim13=2, lim2=2)
            barrier(4, g, is_dma=True)
            ffn_gpsimd(g, 1, skip13=2, skip2=2)

        @block.sync
        def _(sy):
            sy.dma_start(out=xF[:, :, :],
                         in_=xT.rearrange("(cc p) t -> p cc t", p=128)).then_inc(s_xd, 16)
            sy.dma_start(out=ones_row[:, :], in_=ones_d.ap()).then_inc(s_xd, 16)
            sy.dma_start(out=ones_col[:, :], in_=onesc_d.ap()).then_inc(s_xd, 16)
            sy.dma_start(out=b_sb[:, :],
                         in_=bp.rearrange("(cc p) -> p cc", p=128),
                         allow_slow_non_contiguous=True).then_inc(s_xd, 16)
            sy.dma_start(out=idf_sb[:, :], in_=idf_d.ap()).then_inc(s_xd, 16)
            sy.dma_start(out=selm_sb[:, :], in_=selm_d.ap()).then_inc(s_xd, 16)
            barrier(0, sy, is_dma=True, waits=[(s_xd, 6 * 16)])
            barrier(1, sy, is_dma=True)
            sy.dma_start(out=masks[:, :, :], in_=masks_d.ap()).then_inc(s_xd, 16)
            sy.dma_start(out=idb_sb[:, :], in_=idb_d.ap()).then_inc(s_xd, 16)
            sy.dma_start(out=eps64_sb[:, :], in_=eps64_d.ap()).then_inc(s_xd, 16)
            sy.dma_start(out=ones512[:, :], in_=ones512_d.ap()).then_inc(s_xd, 16)
            sy.wait_ge(s_lnx, 2 * LN_TILES * CC)
            sy.dma_start(out=cc1_in.ap().rearrange("(cc p) t -> p cc t", p=128),
                         in_=xn[:, :, :]).then_inc(s_x1d, 16)
            sy.wait_ge(s_cc1, 1)
            for j in range(2):
                sy.dma_start(out=xn1[:, :, j * T:(j + 1) * T],
                             in_=cc1_out.ap()[j].rearrange("(cc p) t -> p cc t", p=128)
                             ).then_inc(s_x1r, 16)
            barrier(2, sy, is_dma=True, waits=[(s_x1r, 32), (s_xd, 10 * 16)])
            for grp in range(NG):
                att_sync(sy, grp)
            barrier(3, sy, is_dma=True, waits=[(s_aod, NG * 16)])
            sy.wait_ge(s_cc2, 1)
            moff = (sy.partition_id() % 2) * T
            for j in range(2):
                sy.dma_start(out=ao2[:, j * 4:(j + 1) * 4, :],
                             in_=cc2_out.ap()[j].rearrange("(cc p) t -> p cc t", p=128)
                             [:, :, bass.ds(moff, T)]).then_inc(s_ao2, 16)
            barrier(4, sy, is_dma=True, waits=[(s_ao2, 32)])
            for fc in range(CC):
                sy.wait_ge(s_tz, 2 * TZ_UNITS - 16 + 2 * (fc + 1))
                sy.dma_start(out=yT.rearrange("(cc p) t -> p cc t", p=128)[:, fc, :],
                             in_=xF[:, fc, :]).then_inc(s_st, 16)
            sy.wait_ge(s_st, 8 * 16)

        @block.tensor
        def _(t):
            barrier(0, t)
            ln_tensor(t, 0)
            gate_tensor(t, 0)
            ffn_tensor(t, 0)
            ln_tensor(t, 1)
            barrier(1, t)
            barrier(2, t)
            for grp in range(NG):
                att_tensor_kvq(t, grp)
                att_tensor_scores(t, grp)
            barrier(3, t)
            barrier(4, t)
            t.wait_ge(s_ao2, 32)
            t.wait_ge(s_pw, 16)
            pslots = [psA[:, 0:512], psA[:, 512:1024], psC[:, 0:512], psC[:, 512:1024]]
            pj = 0
            for oc in range(CC):
                for tt4 in range(4):
                    if pj >= 4:
                        t.wait_ge(s_prv, pj - 3)
                    for cc in range(CC):
                        ins = t.matmul(pslots[pj % 4], wp_sb[:, cc, oc * 128:(oc + 1) * 128],
                                       ao2[:, cc, tt4 * 512:(tt4 + 1) * 512],
                                       start=(cc == 0), stop=(cc == CC - 1))
                    ins.then_inc(s_pr, 1)
                    pj += 1
            ln_tensor(t, 2)
            gate_tensor(t, 1)
            ffn_tensor(t, 1)

        @block.scalar
        def _(s):
            barrier(0, s)
            ln_scalar(s, 0)
            gate_scalar(s, 0)
            ffn_scalar(s, 0)
            ln_scalar(s, 1)
            barrier(1, s)
            barrier(2, s)
            for grp in range(NG):
                att_scalar_scores(s, grp)
            barrier(3, s)
            barrier(4, s)
            ln_scalar(s, 2)
            gate_scalar(s, 1)
            ffn_scalar(s, 1)

        @block.vector
        def _(v):
            barrier(0, v)
            ln_vector(v, 0)
            gate_vector(v, 0)
            ffn_vector(v, 0)
            ln_vector(v, 1)
            barrier(1, v)
            barrier(2, v)
            for grp in range(NG):
                att_vector_kvq(v, grp)
                att_vector_scores(v, grp)
            barrier(3, v)
            barrier(4, v)
            pslots = [psA[:, 0:512], psA[:, 512:1024], psC[:, 0:512], psC[:, 512:1024]]
            pj = 0
            for oc in range(CC):
                for tt4 in range(4):
                    v.wait_ge(s_pr, NG * AO_UNITS + pj + 1)
                    v.scalar_tensor_tensor(xF[:, oc, tt4 * 512:(tt4 + 1) * 512],
                                           pslots[pj % 4], b_sb[:, oc:oc + 1],
                                           xF[:, oc, tt4 * 512:(tt4 + 1) * 512],
                                           ALU.add, ALU.add).then_inc(s_prv, 1)
                    pj += 1
            v.wait_ge(s_prv, 32)
            ln_vector(v, 2)
            gate_vector(v, 1)
            ffn_vector(v, 1)

    return nc


_NC_CACHE = []


def kernel(L0_attn_x, L0_attn_w_qkv, L0_attn_w_proj, L0_attn_b_proj,
           L0_ffn_gate_w, L0_ffn_w1, L0_ffn_w2, L0_ffn_w3,
           L0_ffn_sw1, L0_ffn_sw2, L0_ffn_sw3):
    x = np.asarray(L0_attn_x)
    w_qkv = np.asarray(L0_attn_w_qkv)
    w_proj = np.asarray(L0_attn_w_proj)
    b_proj = np.ascontiguousarray(np.asarray(L0_attn_b_proj))
    gate_w = np.asarray(L0_ffn_gate_w)
    w1 = np.ascontiguousarray(np.asarray(L0_ffn_w1))
    w2 = np.ascontiguousarray(np.asarray(L0_ffn_w2))
    w3 = np.ascontiguousarray(np.asarray(L0_ffn_w3))
    sw1 = np.ascontiguousarray(np.asarray(L0_ffn_sw1))
    sw2 = np.ascontiguousarray(np.asarray(L0_ffn_sw2))
    sw3 = np.ascontiguousarray(np.asarray(L0_ffn_sw3))

    if not _NC_CACHE:
        _NC_CACHE.append(build())
    nc = _NC_CACHE[0]
    gwT = np.ascontiguousarray(gate_w.T)
    wpT = np.ascontiguousarray(w_proj.T)
    in_maps = []
    for core in range(8):
        bb, m = core // 2, core % 2
        xTc = np.ascontiguousarray(x[bb, m * T:(m + 1) * T, :].T)
        wqTc = np.ascontiguousarray(w_qkv[m * 512:(m + 1) * 512, :].T)
        wkTc = np.ascontiguousarray(w_qkv[D + m * 512:D + (m + 1) * 512, :].T)
        wvTc = np.ascontiguousarray(w_qkv[2 * D + m * 512:2 * D + (m + 1) * 512, :].T)
        in_maps.append(dict(xT=xTc, w1=w1, w2=w2, w3=w3, sw1=sw1, sw2=sw2, sw3=sw3,
                            gwT=gwT, wqT=wqTc, wkT=wkTc, wvT=wvTc, wpT=wpT, bp=b_proj))
    res = run_bass_kernel_spmd(nc, in_maps, core_ids=list(range(8)))
    out = np.empty((B, S, D), dtype=np.float32)
    for core in range(8):
        bb, m = core // 2, core % 2
        out[bb, m * T:(m + 1) * T, :] = res.results[core]["yT"].T
    return out


# version 11
# speedup vs baseline: 1.0537x; 1.0537x over previous
import numpy as np
import ml_dtypes
import concourse.bass as bass
import concourse.mybir as mybir
from concourse.bass_utils import run_bass_kernel_spmd

F32 = mybir.dt.float32
BF16 = mybir.dt.bfloat16
AF = mybir.ActivationFunctionType
ALU = mybir.AluOpType
AX = mybir.AxisListType

B, S, D = 4, 4096, 1024
T = 2048
CC = 8
NE = 8
HC = 22
SHC = 12
NG = 4
NQB = 8
NKC = 32
NTP = 2
NTT = 2
EPS = 1e-5
KB = 1024

EXPS = list(range(NE)) + ["sh"]


def _hc(e):
    return SHC if e == "sh" else HC


AC_UNITS = sum(_hc(e) for e in EXPS) * NTP
Y_UNITS = len(EXPS) * CC * NTP
TZ_UNITS = len(EXPS) * CC * NTP * 2
MOE_Q = NE * CC * NTP * 2
W13_SLABS = sum(_hc(e) // 2 for e in EXPS) * NTP
W2_SLABS = len(EXPS) * CC * NTP
LN_TILES = 4
KCS = [4 * (qb + 1) for qb in range(NQB)]
SC_TILES = 2 * sum(KCS)
SC_PAIRS = SC_TILES // 2
AO_UNITS = 2 * NQB


class C:
    pass


def build():
    nc = bass.Bass("TRN2", target_bir_lowering=False, num_devices=8)

    xT = nc.dram_tensor("xT", [D, T], F32, kind="ExternalInput").ap()
    w1 = nc.dram_tensor("w1", [NE, D, HC * 128], F32, kind="ExternalInput").ap()
    w2 = nc.dram_tensor("w2", [NE, HC * 128, D], F32, kind="ExternalInput").ap()
    w3 = nc.dram_tensor("w3", [NE, D, HC * 128], F32, kind="ExternalInput").ap()
    sw1 = nc.dram_tensor("sw1", [D, SHC * 128], F32, kind="ExternalInput").ap()
    sw2 = nc.dram_tensor("sw2", [SHC * 128, D], F32, kind="ExternalInput").ap()
    sw3 = nc.dram_tensor("sw3", [D, SHC * 128], F32, kind="ExternalInput").ap()
    gwT = nc.dram_tensor("gwT", [D, NE], F32, kind="ExternalInput").ap()
    wqT = nc.dram_tensor("wqT", [D, 512], F32, kind="ExternalInput").ap()
    wkT = nc.dram_tensor("wkT", [D, 512], F32, kind="ExternalInput").ap()
    wvT = nc.dram_tensor("wvT", [D, 512], F32, kind="ExternalInput").ap()
    wpT = nc.dram_tensor("wpT", [D, D], F32, kind="ExternalInput").ap()
    bp = nc.dram_tensor("bp", [D], F32, kind="ExternalInput").ap()
    yT = nc.dram_tensor("yT", [D, T], F32, kind="ExternalOutput").ap()

    cc1_in = nc.dram_tensor("cc1_in", [D, T], BF16)
    cc1_out = nc.dram_tensor("cc1_out", [2, D, T], BF16)
    cc2_in = nc.dram_tensor("cc2_in", [512, S], BF16)
    cc2_out = nc.dram_tensor("cc2_out", [2, 512, S], BF16)

    mask_np = np.zeros((4, 128, 512), dtype=ml_dtypes.bfloat16)
    bigv = 3.3895314e38
    for a in range(4):
        kk = np.arange(128)[:, None]
        qq = np.arange(512)[None, :]
        mask_np[a] = ((kk <= qq - 128 * a) * bigv).astype(ml_dtypes.bfloat16)
    masks_d = nc.inline_tensor(np.ascontiguousarray(mask_np.transpose(1, 0, 2)),
                               name="cmasks")
    ones_d = nc.inline_tensor(np.ones((1, 128), np.float32), name="cones")
    onesc_d = nc.inline_tensor(np.ones((128, 1), dtype=ml_dtypes.bfloat16), name="conesc")
    idf_d = nc.inline_tensor(np.eye(128, dtype=np.float32), name="cidf")
    eps64_np = np.zeros((1, 65), dtype=ml_dtypes.bfloat16); eps64_np[0, 64] = 1.0
    eps64_d = nc.inline_tensor(eps64_np, name="ceps64")
    ones512_d = nc.inline_tensor(np.full((1, 512), 1e-9, dtype=ml_dtypes.bfloat16),
                                 name="cones512")
    selm_np = np.zeros((NE, NE * 128), np.float32)
    for e in range(NE):
        selm_np[e, e * 128:(e + 1) * 128] = 1.0
    selm_d = nc.inline_tensor(selm_np, name="cselm")
    idb_d = nc.inline_tensor(np.eye(128, dtype=ml_dtypes.bfloat16), name="cidb")

    def at(name, shape, dtype, off):
        return nc.alloc_sbuf_tensor_at(name, shape, dtype, offset=off)

    MB = 17920
    xF = at("xF", [128, CC, T], F32, MB)
    A0 = MB + 64 * KB
    xn = at("xn", [128, CC, T], BF16, A0)
    h = at("h", [128, HC, 1024], BF16, A0 + 32 * KB)
    w1s = at("w1s", [128, 2, CC, 2, 128], BF16, A0 + 76 * KB)
    w3s = at("w3s", [128, 2, CC, 2, 128], BF16, A0 + 84 * KB)
    w2s = at("w2s", [128, 2, HC, 128], BF16, A0 + 92 * KB)
    cbt = at("cbt", [128, T], F32, A0 + 105 * KB + 512)
    lg = at("lg", [8, T], F32, A0 + 105 * KB + 512)
    tq = at("tq", [128, 2, 512], F32, A0 + 113 * KB + 512)
    combT = at("combT", [NE, T], F32, A0 + 117 * KB + 512)
    xb = at("xb", [128, 2, 512], BF16, A0 + 125 * KB + 512)
    sqb = at("sqb", [128, 2, 512], BF16, A0 + 127 * KB + 512)
    gt = at("gt", [128, 16, NE], F32, A0 + 129 * KB + 512)
    gs = at("gs", [128, 16, 4], F32, A0 + 130 * KB)
    gmsk = at("gmsk", [128, 16, NE], F32, A0 + 130 * KB + 256)
    selm_sb = at("selm_sb", [NE, NE * 128], F32, A0 + 130 * KB + 768)
    FFN_HIGH = A0 + 134 * KB + 768

    xn1 = at("xn1", [128, CC, S], BF16, A0)
    AT = A0 + 64 * KB
    Kg = at("Kg", [128, S], BF16, AT)
    Vg = at("Vg", [128, NKC, 130], BF16, AT + 8 * KB)
    Qg = at("Qg", [128, S], BF16, AT + 16 * KB + 512)
    Vf = at("Vf", [128, 2, 512], BF16, AT + 24 * KB + 512)
    ao = at("ao", [128, S], BF16, AT + 26 * KB + 512)
    wq_sb = at("wq_sb", [128, CC, 128], BF16, AT + 34 * KB + 512)
    wk_sb = at("wk_sb", [128, CC, 128], BF16, AT + 36 * KB + 512)
    wv_sb = at("wv_sb", [128, CC, 128], BF16, AT + 38 * KB + 512)
    e_scr = at("e_scr", [128, 2, 1024], BF16, AT + 40 * KB + 512)
    eps64_sb = at("eps64_sb", [1, 65], BF16, AT + 44 * KB + 512)
    ones512 = at("ones512", [1, 512], BF16, AT + 44 * KB + 512 + 160)
    sp = at("sp", [128, 4, 512], BF16, AT + 48 * KB + 512)
    pp = at("pp", [128, 4, 512], BF16, AT + 52 * KB + 512)
    msp = at("msp", [128, 4, 512], BF16, AT + 56 * KB + 512)
    rbv = at("rbv", [128, 2, 512], BF16, AT + 60 * KB + 512)
    masks = at("cm_sb", [128, 4, 512], BF16, AT + 62 * KB + 512)
    idb_sb = at("idb_sb", [128, 128], BF16, AT + 66 * KB + 512)
    ATT_HIGH = AT + 66 * KB + 768
    assert ATT_HIGH <= A0 + 130 * KB + 768, (ATT_HIGH, A0)
    ao2 = at("ao2", [128, CC, T], BF16, A0)
    wp_sb = at("wp_sb", [128, CC, 1024], BF16, A0 + 32 * KB)

    ST = FFN_HIGH
    ones_row = at("ones_row", [1, 128], F32, ST)
    ones_col = at("ones_col", [128, 1], BF16, ST + 512)
    b_sb = at("b_sb", [128, CC], F32, ST + 544)
    gw_sb = at("gw_sb", [128, CC, NE], BF16, ST + 576)
    idf_sb = at("idf_sb", [128, 128], F32, ST + 704)
    st_m = at("st_m", [1, 512], F32, ST + 1216)
    st_r = at("st_r", [1, 512], F32, ST + 3264)
    st_t = at("st_t", [1, 512], F32, ST + 5312)
    assert ST + 7360 <= 229376, (ST,)

    psA = nc.alloc_psum_tensor("psA", [128, 1024], F32)
    psC = nc.alloc_psum_tensor("psC", [128, 1024], F32)
    psY = nc.alloc_psum_tensor("psY", [128, 1024], F32)
    psS = nc.alloc_psum_tensor("psS", [128, 512], F32)
    pt = nc.alloc_psum_tensor("pt", [128, 512], BF16)

    sem = nc.alloc_semaphore
    s_xd = sem("s_xd")
    s_gw = sem("s_gw")
    s_w13 = [sem("s_w13_0"), sem("s_w13_1")]
    s_w2 = [sem("s_w2_0"), sem("s_w2_1")]
    s_te, s_ta, s_tv, s_ty, s_tz, s_tq = (sem("s_te"), sem("s_ta"), sem("s_tv"),
                                          sem("s_ty"), sem("s_tz"), sem("s_tq"))
    s_ln, s_lnm, s_lne, s_lnb, s_lnx = (sem("s_ln"), sem("s_lnm"), sem("s_lne"),
                                        sem("s_lnb"), sem("s_lnx"))
    s_lnc, s_lnv, s_lns, s_lnq = sem("s_lnc"), sem("s_lnv"), sem("s_lns"), sem("s_lnq")
    s_lnp = sem("s_lnp")
    s_gl, s_glv, s_gtr, s_gte = sem("s_gl"), sem("s_glv"), sem("s_gtr"), sem("s_gte")
    s_gop, s_gx = sem("s_gop"), sem("s_gx")
    s_cbm, s_cbv = sem("s_cbm"), sem("s_cbv")
    s_cc1, s_cc2 = sem("s_cc1"), sem("s_cc2")
    s_x1d, s_x1r = sem("s_x1d"), sem("s_x1r")
    s_kv, s_kve, s_vt, s_vte, s_aw = (sem("s_kv"), sem("s_kve"), sem("s_vt"),
                                      sem("s_vte"), sem("s_aw"))
    s_tes, s_tae, s_tal, s_tvp, s_msp = (sem("s_tes"), sem("s_tae"), sem("s_tal"),
                                         sem("s_tvp"), sem("s_msp"))
    s_tpv, s_den, s_tve = sem("s_tpv"), sem("s_den"), sem("s_tve")
    s_rcl = sem("s_rcl")
    s_pr, s_prv, s_aod, s_pw, s_ao2 = (sem("s_pr"), sem("s_prv"), sem("s_aod"),
                                       sem("s_pw"), sem("s_ao2"))
    s_st = sem("s_st")
    s_ph = [sem(f"s_ph{i}") for i in range(8)]

    slab13_consumed = {}
    slab2_consumed = {}
    _te = 0
    _ty = 0
    _i13 = 0
    _i2 = 0
    for call in range(2):
        for e in EXPS:
            hcn = _hc(e)
            for tp in range(NTP):
                for hcp in range(hcn // 2):
                    slab13_consumed[_i13] = _te + (2 * hcp + 2)
                    _i13 += 1
                _te += hcn
                for fc in range(CC):
                    slab2_consumed[_i2] = _ty + 1
                    _ty += 1
                    _i2 += 1

    def fb(call):
        return dict(te=call * AC_UNITS, ty=call * Y_UNITS, tz=call * TZ_UNITS,
                    tq=call * MOE_Q, i13=call * W13_SLABS, i2=call * W2_SLABS,
                    cbm=call * NE * 4)

    def lnb(k):
        return dict(ln=k * LN_TILES * CC, lnm=k * LN_TILES, lne=k * LN_TILES,
                    lnb=k * LN_TILES, lnx=k * LN_TILES * CC, lnc=k * LN_TILES * CC,
                    lnv=k * LN_TILES, lns=k * LN_TILES, lnq=k * LN_TILES * CC,
                    lnp=k * LN_TILES * CC)

    def gb(call):
        return dict(gl=call * 4, glv=call * 4, gtr=call * 32, gte=call * 32,
                    gop=call * 11 * 16, gx=call * 16)

    def barrier(i, eng, is_dma=False, waits=()):
        for (ss, vv) in waits:
            eng.wait_ge(ss, vv)
        if not is_dma:
            eng.drain()
        eng.sem_inc(s_ph[i], 1)
        eng.wait_ge(s_ph[i], 5)

    def ln_tensor(t, k):
        b = lnb(k)
        if k == 1:
            t.wait_ge(s_tz, TZ_UNITS)
            t.wait_ge(s_cbv, 32)
        for i in range(LN_TILES):
            for cc in range(CC):
                t.wait_ge(s_ln, b["ln"] + i * CC + cc + 1)
                t.matmul(psS[0:1, :], ones_col[:, 0:1], xb[:, cc % 2, :],
                         start=(cc == 0), stop=(cc == CC - 1))
                t.matmul(psY[0:1, 0:512], ones_col[:, 0:1], sqb[:, cc % 2, :],
                         start=(cc == 0), stop=(cc == CC - 1)).then_inc(s_lnp, 1)
            t.wait_ge(s_lne, b["lne"] + i + 1)
            t.matmul(psA[:, 0:512], ones_row[0:1, :], st_r[0:1, :], start=True, stop=True)
            t.matmul(psC[:, 0:512], ones_row[0:1, :], st_m[0:1, :],
                     start=True, stop=True).then_inc(s_lnb, 1)
            if i < LN_TILES - 1:
                t.wait_ge(s_lnx, b["lnx"] + (i + 1) * CC)

    def ln_scalar(s, k):
        b = lnb(k)
        for i in range(LN_TILES):
            s.wait_ge(s_lnv, b["lnv"] + i + 1)
            s.activation(st_r[0:1, :], st_t[0:1, :], AF.Sqrt).then_inc(s_lns, 1)

    def ln_vector(v, k):
        b = lnb(k)
        for i in range(LN_TILES):
            tok = i * 512
            for cc in range(CC + 1):
                if cc < CC:
                    idx = i * CC + cc
                    if idx >= 2:
                        v.wait_ge(s_lnp, b["lnp"] + idx - 1)
                    v.tensor_copy(xb[:, cc % 2, :], xF[:, cc, tok:tok + 512]).then_inc(s_lnc, 1)
                if cc >= 1:
                    ccp = cc - 1
                    v.wait_ge(s_lnc, b["lnc"] + i * CC + ccp + 1)
                    v.tensor_tensor(sqb[:, ccp % 2, :], xb[:, ccp % 2, :],
                                    xb[:, ccp % 2, :], ALU.mult).then_inc(s_ln, 1)
            v.wait_ge(s_lnp, b["lnp"] + (i + 1) * CC)
            v.wait_ge(s_lns, b["lns"] + i)
            v.tensor_scalar(st_m[0:1, :], psS[0:1, :], 1.0 / D, None, ALU.mult)
            v.tensor_scalar(st_t[0:1, :], psY[0:1, 0:512], 1.0 / D, None, ALU.mult)
            v.drain()
            v.tensor_tensor(st_r[0:1, :], st_m[0:1, :], st_m[0:1, :], ALU.mult)
            v.drain()
            v.tensor_tensor(st_t[0:1, :], st_t[0:1, :], st_r[0:1, :], ALU.subtract)
            v.drain()
            v.tensor_scalar(st_t[0:1, :], st_t[0:1, :], EPS, None,
                            ALU.add).then_inc(s_lnv, 1)
            v.wait_ge(s_lns, b["lns"] + i + 1)
            v.reciprocal(st_r[0:1, :], st_r[0:1, :])
            v.drain()
            v.tensor_tensor(st_m[0:1, :], st_m[0:1, :], st_r[0:1, :],
                            ALU.mult).then_inc(s_lne, 1)
            v.wait_ge(s_lnb, b["lnb"] + i + 1)
            for cc in range(CC + 1):
                if cc < CC:
                    if i * CC + cc >= 2:
                        v.wait_ge(s_lnx, b["lnx"] + i * CC + cc - 1)
                    v.tensor_tensor(tq[:, cc % 2, :], xF[:, cc, tok:tok + 512],
                                    psA[:, 0:512], ALU.mult).then_inc(s_lnq, 1)
                if cc >= 1:
                    ccp = cc - 1
                    v.wait_ge(s_lnq, b["lnq"] + i * CC + ccp + 1)
                    v.tensor_tensor(xn[:, ccp, tok:tok + 512], tq[:, ccp % 2, :],
                                    psC[:, 0:512], ALU.subtract).then_inc(s_lnx, 1)

    def gate_tensor(t, call):
        b = gb(call)
        bl = lnb(2 * call)
        t.wait_ge(s_lnx, bl["lnx"] + LN_TILES * CC)
        for i in range(4):
            if i > 0:
                t.wait_ge(s_glv, b["glv"] + i)
            for cc in range(CC):
                ins = t.matmul(psS[0:NE, :], gw_sb[:, cc, :],
                               xn[:, cc, i * 512:(i + 1) * 512],
                               start=(cc == 0), stop=(cc == CC - 1))
            ins.then_inc(s_gl, 1)
        t.wait_ge(s_glv, b["glv"] + 4)
        for j in range(16):
            if j > 0:
                t.wait_ge(s_gte, b["gte"] + j)
            t.transpose(psS[:, 0:NE], lg[0:NE, j * 128:(j + 1) * 128],
                        idf_sb[0:NE, 0:NE]).then_inc(s_gtr, 1)
        t.wait_ge(s_gop, b["gop"] + 11 * 16)
        for j in range(16):
            if j > 0:
                t.wait_ge(s_gte, b["gte"] + 16 + j)
            t.transpose(psS[0:NE, 0:128], gt[:, j, :],
                        idf_sb[:, :]).then_inc(s_gtr, 1)

    def gate_scalar(s, call):
        b = gb(call)
        for j in range(16):
            s.wait_ge(s_gop, b["gop"] + j + 1)
            s.activation(gt[:, j, :], gt[:, j, :], AF.Exp,
                         bias=gs[:, j, 0:1]).then_inc(s_gx, 1)

    def gate_vector(v, call):
        b = gb(call)
        for i in range(4):
            v.wait_ge(s_gl, b["gl"] + i + 1)
            v.tensor_copy(lg[0:NE, i * 512:(i + 1) * 512], psS[0:NE, :]).then_inc(s_glv, 1)
        for j in range(16):
            v.wait_ge(s_gtr, b["gtr"] + j + 1)
            v.tensor_copy(gt[:, j, :], psS[:, 0:NE]).then_inc(s_gte, 1)
        gop = b["gop"]

        for j in range(16):
            v.wait_ge(s_gte, b["gte"] + j + 1)
            v.reduce_max(gs[:, j, 0:1], gt[:, j, :], axis=AX.X,
                         negate=True).then_inc(s_gop, 1)
        gop += 16
        for j in range(16):
            v.wait_ge(s_gx, b["gx"] + j + 1)
            v.reduce_sum(gs[:, j, 1:2], gt[:, j, :], axis=AX.X).then_inc(s_gop, 1)
        gop += 16

        def step(fn):
            nonlocal gop
            for j in range(16):
                v.wait_ge(s_gop, gop - 16 + j + 1)
                fn(j).then_inc(s_gop, 1)
            gop += 16

        step(lambda j: v.reciprocal(gs[:, j, 2:3], gs[:, j, 1:2]))
        step(lambda j: v.tensor_scalar(gt[:, j, :], gt[:, j, :], gs[:, j, 2:3],
                                       None, ALU.mult))
        step(lambda j: v.reduce_max(gs[:, j, 3:4], gt[:, j, :], axis=AX.X))
        step(lambda j: v.tensor_scalar(gmsk[:, j, :], gt[:, j, :], gs[:, j, 3:4],
                                       None, ALU.is_lt))
        step(lambda j: v.tensor_tensor(gmsk[:, j, :], gmsk[:, j, :], gt[:, j, :],
                                       ALU.mult))
        step(lambda j: v.reduce_max(gs[:, j, 3:4], gmsk[:, j, :], axis=AX.X))
        step(lambda j: v.tensor_scalar(gmsk[:, j, :], gt[:, j, :], gs[:, j, 3:4],
                                       None, ALU.is_ge))
        step(lambda j: v.tensor_tensor(gt[:, j, :], gt[:, j, :], gmsk[:, j, :],
                                       ALU.mult))
        step(lambda j: v.tensor_scalar(gt[:, j, :], gt[:, j, :], 0.5, None, ALU.mult))
        for j in range(16):
            v.wait_ge(s_gtr, b["gtr"] + 16 + j + 1)
            v.tensor_copy(combT[0:NE, j * 128:(j + 1) * 128],
                          psS[0:NE, 0:128]).then_inc(s_gte, 1)

    def ffn_gpsimd(g, call, lim13=None, lim2=None, skip13=0, skip2=0):
        b = fb(call)
        i13 = b["i13"]
        i2 = b["i2"]
        n13 = 0
        n2 = 0
        for e in EXPS:
            hcn = _hc(e)
            for tp in range(NTP):
                for hcp in range(hcn // 2):
                    n13 += 1
                    if (lim13 is not None and n13 > lim13) or n13 <= skip13:
                        i13 += 1
                        continue
                    buf = i13 % 2
                    if i13 >= 2:
                        g.wait_ge(s_te, slab13_consumed[i13 - 2])
                    if e == "sh":
                        sA = sw1.rearrange("(cc p) hh -> p cc hh", p=128)
                        sB = sw3.rearrange("(cc p) hh -> p cc hh", p=128)
                    else:
                        sA = w1[e].rearrange("(cc p) hh -> p cc hh", p=128)
                        sB = w3[e].rearrange("(cc p) hh -> p cc hh", p=128)
                    g.dma_start(out=w1s[:, buf, :, :, :].rearrange("p cc x w -> p cc (x w)"),
                                in_=sA[:, :, hcp * 256:(hcp + 1) * 256]).then_inc(s_w13[buf], 16)
                    g.dma_start(out=w3s[:, buf, :, :, :].rearrange("p cc x w -> p cc (x w)"),
                                in_=sB[:, :, hcp * 256:(hcp + 1) * 256]).then_inc(s_w13[buf], 16)
                    i13 += 1
                for fc in range(CC):
                    n2 += 1
                    if (lim2 is not None and n2 > lim2) or n2 <= skip2:
                        i2 += 1
                        continue
                    buf = i2 % 2
                    if i2 >= 2:
                        g.wait_ge(s_ty, slab2_consumed[i2 - 2])
                    if e == "sh":
                        src2 = sw2.rearrange("(hc p) f -> p hc f", p=128)[
                            :, :, fc * 128:(fc + 1) * 128]
                        dst2 = w2s[:, buf, 0:SHC, :]
                    else:
                        src2 = w2[e].rearrange("(hc p) f -> p hc f", p=128)[
                            :, :, fc * 128:(fc + 1) * 128]
                        dst2 = w2s[:, buf, :, :]
                    g.dma_start(out=dst2, in_=src2).then_inc(s_w2[buf], 16)
                    i2 += 1

    def ffn_tensor(t, call):
        b = fb(call)
        uid = b["te"]
        yid = b["ty"]
        tzc = b["tz"]
        i13 = b["i13"]
        i2 = b["i2"]
        cbm = b["cbm"]
        for ei, e in enumerate(EXPS):
            hcn = _hc(e)
            for tp in range(NTP):
                tok = tp * 1024
                t.wait_ge(s_tz, tzc)
                for hc in range(hcn):
                    hcp, hcx = divmod(hc, 2)
                    slab = i13 + hcp
                    buf = slab % 2
                    if hcx == 0:
                        t.wait_ge(s_w13[buf], (slab // 2 + 1) * 32)
                    ab = psA if uid % 2 == 0 else psY
                    if uid >= 2:
                        t.wait_ge(s_ta, uid - 1)
                    for cc in range(CC):
                        t.matmul(ab[:, 0:512], w1s[:, buf, cc, hcx, :],
                                 xn[:, cc, tok:tok + 512],
                                 start=(cc == 0), stop=(cc == CC - 1))
                        t.matmul(ab[:, 512:1024], w1s[:, buf, cc, hcx, :],
                                 xn[:, cc, tok + 512:tok + 1024],
                                 start=(cc == 0), stop=(cc == CC - 1))
                    if uid >= 1:
                        t.wait_ge(s_tv, uid)
                    for cc in range(CC):
                        t.matmul(psC[:, 0:512], w3s[:, buf, cc, hcx, :],
                                 xn[:, cc, tok:tok + 512],
                                 start=(cc == 0), stop=(cc == CC - 1))
                        ins = t.matmul(psC[:, 512:1024], w3s[:, buf, cc, hcx, :],
                                       xn[:, cc, tok + 512:tok + 1024],
                                       start=(cc == 0), stop=(cc == CC - 1))
                    ins.then_inc(s_te, 1)
                    uid += 1
                i13 += hcn // 2
                if tp == 0 and e != "sh":
                    t.wait_ge(s_gte, gb(call)["gte"] + 32)
                    for j in range(4):
                        if cbm > 0:
                            t.wait_ge(s_cbv, cbm)
                        t.matmul(psS[:, :], selm_sb[:, e * 128:(e + 1) * 128],
                                 combT[0:NE, j * 512:(j + 1) * 512],
                                 start=True, stop=True).then_inc(s_cbm, 1)
                        cbm += 1
                t.wait_ge(s_tv, uid)
                for fc in range(CC):
                    slab2 = i2 + fc
                    buf = slab2 % 2
                    t.wait_ge(s_w2[buf], (slab2 // 2 + 1) * 16)
                    yb = psY if (yid % 2 == 0) else psA
                    if yid >= 2:
                        t.wait_ge(s_tz, tzc - 2)
                    for hc in range(hcn):
                        t.matmul(yb[:, 0:512], w2s[:, buf, hc, :],
                                 h[:, hc, 0:512],
                                 start=(hc == 0), stop=(hc == hcn - 1))
                        ins = t.matmul(yb[:, 512:1024], w2s[:, buf, hc, :],
                                       h[:, hc, 512:1024],
                                       start=(hc == 0), stop=(hc == hcn - 1))
                    ins.then_inc(s_ty, 1)
                    yid += 1
                    tzc += 2
                i2 += CC

    def ffn_scalar(s, call):
        b = fb(call)
        uid = b["te"]
        yid = b["ty"]
        for ei, e in enumerate(EXPS):
            hcn = _hc(e)
            for tp in range(NTP):
                if ei > 0 or tp > 0:
                    s.wait_ge(s_ty, yid)
                yid += CC
                for hc in range(hcn):
                    s.wait_ge(s_te, uid + 1)
                    ab = psA if uid % 2 == 0 else psY
                    s.activation(h[:, hc, :], ab[:, :], AF.Silu).then_inc(s_ta, 1)
                    uid += 1

    def ffn_vector(v, call):
        b = fb(call)
        uid = b["te"]
        yid = b["ty"]
        qid = b["tq"]
        tzc = b["tz"]
        cbv = b["cbm"]
        pend = None
        tz_of_add = {}

        def flush():
            nonlocal pend, tzc
            if pend is not None:
                v.wait_ge(s_tq, pend[0] + 1)
                v.tensor_tensor(pend[1], pend[1], tq[:, pend[0] % 2, :],
                                ALU.add).then_inc(s_tz, 1)
                tzc += 1
                tz_of_add[pend[0]] = tzc
                pend = None

        for ei, e in enumerate(EXPS):
            hcn = _hc(e)
            for tp in range(NTP):
                tok = tp * 1024
                flush()
                if ei > 0 or tp > 0:
                    v.wait_ge(s_ty, yid)
                for hc in range(hcn):
                    v.wait_ge(s_ta, uid + 1)
                    v.tensor_tensor(h[:, hc, :], h[:, hc, :], psC[:, :],
                                    ALU.mult).then_inc(s_tv, 1)
                    uid += 1
                if tp == 0 and e != "sh":
                    flush()
                    v.wait_ge(s_tz, tzc)
                    for j in range(4):
                        v.wait_ge(s_cbm, cbv + j + 1)
                        v.tensor_copy(cbt[:, j * 512:(j + 1) * 512], psS[:, :]).then_inc(s_cbv, 1)
                    cbv += 4
                for fc in range(CC):
                    yb = psY if (yid % 2 == 0) else psA
                    v.wait_ge(s_ty, yid + 1)
                    for half in range(2):
                        tk = tok + half * 512
                        if e == "sh":
                            flush()
                            v.scalar_tensor_tensor(xF[:, fc, tk:tk + 512],
                                                   yb[:, half * 512:(half + 1) * 512], 0.5,
                                                   xF[:, fc, tk:tk + 512],
                                                   ALU.mult, ALU.add).then_inc(s_tz, 1)
                            tzc += 1
                        else:
                            if qid >= 2 and (qid - 2) in tz_of_add:
                                v.wait_ge(s_tz, tz_of_add[qid - 2])
                            v.tensor_tensor(tq[:, qid % 2, :],
                                            yb[:, half * 512:(half + 1) * 512],
                                            cbt[:, tk:tk + 512], ALU.mult).then_inc(s_tq, 1)
                            flush()
                            pend = (qid, xF[:, fc, tk:tk + 512])
                            qid += 1
                    yid += 1
        flush()

    def att_gpsimd(g, grp):
        if grp >= 1:
            g.wait_ge(s_kv, 24 * grp)
        for w_in, w_out in ((wqT, wq_sb), (wkT, wk_sb), (wvT, wv_sb)):
            g.dma_start(out=w_out[:, :, :],
                        in_=w_in.rearrange("(cc p) f -> p cc f", p=128)[
                            :, :, grp * 128:(grp + 1) * 128]).then_inc(s_aw, 16)

    def att_tensor_kvq(t, grp):
        t.wait_ge(s_aw, (grp + 1) * 48)
        if grp >= 1:
            t.wait_ge(s_tae, grp * SC_PAIRS)
        slots = [psA[:, 0:512], psA[:, 512:1024], psC[:, 0:512], psC[:, 512:1024]]
        kv = grp * 24
        vt = grp * 32
        for tt in range(8):
            if kv >= 4:
                t.wait_ge(s_kve, kv - 3)
            for cc in range(CC):
                ins = t.matmul(slots[kv % 4], wk_sb[:, cc, :],
                               xn1[:, cc, tt * 512:(tt + 1) * 512],
                               start=(cc == 0), stop=(cc == CC - 1))
            ins.then_inc(s_kv, 1)
            kv += 1
        for tt in range(8):
            if kv >= 4:
                t.wait_ge(s_kve, kv - 3)
            for cc in range(CC):
                ins = t.matmul(slots[kv % 4], wv_sb[:, cc, :],
                               xn1[:, cc, tt * 512:(tt + 1) * 512],
                               start=(cc == 0), stop=(cc == CC - 1))
            ins.then_inc(s_kv, 1)
            kv += 1
            for j in range(4):
                t.wait_ge(s_kve, kv)
                if vt >= 1:
                    t.wait_ge(s_vte, vt)
                t.transpose(pt[:, 0:128],
                            Vf[:, tt % 2, j * 128:(j + 1) * 128],
                            idb_sb[:, :]).then_inc(s_vt, 1)
                vt += 1
        for tt in range(8):
            if kv >= 4:
                t.wait_ge(s_kve, kv - 3)
            for cc in range(CC):
                ins = t.matmul(slots[kv % 4], wq_sb[:, cc, :],
                               xn1[:, cc, tt * 512:(tt + 1) * 512],
                               start=(cc == 0), stop=(cc == CC - 1))
            ins.then_inc(s_kv, 1)
            kv += 1

    def att_vector_kvq(v, grp):
        slots = [psA[:, 0:512], psA[:, 512:1024], psC[:, 0:512], psC[:, 512:1024]]
        kv = grp * 24
        vt = grp * 32
        if grp >= 1:
            v.wait_ge(s_tpv, grp * AO_UNITS)
        v.memset(Vg.ap().rearrange("p k (x f) -> p k x f", x=2)[:, :, :, 64:65], 1.0)
        for tt in range(8):
            v.wait_ge(s_kv, kv + 1)
            v.tensor_copy(Kg[:, tt * 512:(tt + 1) * 512], slots[kv % 4]).then_inc(s_kve, 1)
            kv += 1
        for tt in range(8):
            v.wait_ge(s_kv, kv + 1)
            v.tensor_copy(Vf[:, tt % 2, :], slots[kv % 4]).then_inc(s_kve, 1)
            kv += 1
            for j in range(4):
                kc = tt * 4 + j
                v.wait_ge(s_vt, vt + 1)
                v.tensor_copy(
                    Vg[:, kc, :].rearrange("p (x f) -> p x f", x=2)[:, :, 0:64],
                    pt[:, 0:128].rearrange("p (x f) -> p x f", x=2)[:, :, 0:64],
                ).then_inc(s_vte, 1)
                vt += 1
        for tt in range(8):
            v.wait_ge(s_kv, kv + 1)
            v.tensor_copy(Qg[:, tt * 512:(tt + 1) * 512], slots[kv % 4]).then_inc(s_kve, 1)
            kv += 1

    def att_pairs():
        pairs = []
        for qb in range(NQB):
            L = KCS[qb]
            for pu in range(L // 2):
                for h2 in range(2):
                    pairs.append((h2, qb, 2 * pu, pu == 0, pu == L // 2 - 1,
                                  2 * qb + h2))
        return pairs

    def att_tensor_scores(t, grp):
        pj0 = grp * SC_PAIRS
        gu0 = grp * AO_UNITS
        t.wait_ge(s_kve, (grp + 1) * 24)
        t.wait_ge(s_vte, (grp + 1) * 32)
        pairs = att_pairs()
        pbank = [psA, psC]
        pend_den = []

        def flush_den(now_j):
            while pend_den and (now_j is None or pend_den[0][0] <= now_j - 2):
                _, gu = pend_den.pop(0)
                t.wait_ge(s_den, gu + 1)
                if gu >= 1:
                    t.wait_ge(s_tve, gu)
                t.matmul(psS[0:64, :], ones_row[0:1, 0:64], st_r[0:1, :],
                         start=True, stop=True).then_inc(s_pr, 1)

        def emit_score_pair(j):
            h2, qb, kc, _, _, _ = pairs[j]
            gpj = pj0 + j
            if gpj >= 2:
                t.wait_ge(s_tae, gpj - 1)
            bk = pbank[gpj % 2]
            for half in range(2):
                ins = t.matmul(bk[:, half * 512:(half + 1) * 512],
                               Kg[h2 * 64:(h2 + 1) * 64,
                                  (kc + half) * 128:(kc + half + 1) * 128],
                               Qg[h2 * 64:(h2 + 1) * 64, qb * 512:(qb + 1) * 512],
                               start=True, stop=True)
            ins.then_inc(s_tes, 1)

        emit_score_pair(0)
        for j, (h2, qb, kc, u_start, u_end, lu) in enumerate(pairs):
            gpj = pj0 + j
            gu = gu0 + lu
            yh = psY[:, h2 * 512:(h2 + 1) * 512]
            flush_den(j)
            if u_start:
                if gu >= 2:
                    t.wait_ge(s_tve, gu - 1)
                t.matmul(yh[0:65, :], eps64_sb[0:1, :], ones512[0:1, :],
                         start=True, stop=False)
            if j + 1 < len(pairs):
                emit_score_pair(j + 1)
            t.wait_ge(s_tvp, gpj + 1)
            for half in range(2):
                ins = t.matmul(yh[0:65, :],
                               Vg[:, kc + half, h2 * 65:(h2 + 1) * 65],
                               pp[:, (2 * gpj + half) % 4, :],
                               start=False, stop=(u_end and half == 1))
            if u_end:
                ins.then_inc(s_tpv, 1)
                pend_den.append((j, gu))
        flush_den(None)
        return

    def att_scalar_scores(s, grp):
        pj0 = grp * SC_PAIRS
        gu0 = grp * AO_UNITS
        pbank = [psA, psC]
        pairs = att_pairs()
        pend_rc = []

        def flush_rc(now_j):
            while pend_rc and (now_j is None or pend_rc[0][0] <= now_j - 2):
                _, gu, o2 = pend_rc.pop(0)
                if gu >= 1:
                    s.wait_ge(s_den, gu)
                s.wait_ge(s_tpv, gu + 1)
                s.activation(st_t[0:1, :], psY[64:65, o2 * 512:(o2 + 1) * 512],
                             AF.Ln).then_inc(s_rcl, 1)
                if gu >= 1:
                    s.wait_ge(s_pr, gu)
                s.wait_ge(s_rcl, gu + 1)
                s.activation(st_r[0:1, :], st_t[0:1, :], AF.Exp,
                             scale=-1.0).then_inc(s_den, 1)

        for j, (h2, qb, kc, u_start, u_end, lu) in enumerate(pairs):
            gpj = pj0 + j
            s.wait_ge(s_tes, gpj + 1)
            if gpj >= 2:
                s.wait_ge(s_tal, gpj - 1)
            s.activation(e_scr[:, gpj % 2, :], pbank[gpj % 2][:, :], AF.Exp,
                         scale=0.125).then_inc(s_tae, 1)
            s.wait_ge(s_tae, gpj + 1)
            if gpj >= 2:
                s.wait_ge(s_tvp, gpj - 1)
            s.activation(sp[:, (2 * gpj) % 4:(2 * gpj) % 4 + 2, :]
                         .rearrange("p a b -> p (a b)"),
                         e_scr[:, gpj % 2, :],
                         AF.Ln, bias=1.0).then_inc(s_tal, 1)
            if u_end:
                pend_rc.append((j, gu0 + lu, h2))
            flush_rc(j)
        flush_rc(None)

    def att_vector_scores(v, grp):
        pj0 = grp * SC_PAIRS
        gu0 = grp * AO_UNITS
        mc = grp * AO_UNITS * 2
        if grp >= 1:
            v.wait_ge(s_aod, grp * 16)
        pairs = att_pairs()
        np_us = {}
        for h2, qb, kc, u_start, u_end, lu in pairs:
            np_us[lu] = KCS[qb] // 2
        pend_ep = []

        def flush_ep(now_j):
            while pend_ep and (now_j is None or pend_ep[0][0] <= now_j - 2):
                _, gu, h2, qb = pend_ep.pop(0)
                o2 = h2
                v.wait_ge(s_pr, gu + 1)
                if gu >= 2:
                    v.wait_ge(s_tve, gu - 1)
                v.tensor_copy(rbv[0:64, o2, :], psS[0:64, :])
                v.drain()
                v.tensor_tensor(ao[h2 * 64:(h2 + 1) * 64, qb * 512:(qb + 1) * 512],
                                psY[0:64, o2 * 512:(o2 + 1) * 512], rbv[0:64, o2, :],
                                ALU.mult).then_inc(s_tve, 1)

        mspc = {}
        for j, (h2, qb, kc, u_start, u_end, lu) in enumerate(pairs):
            gpj = pj0 + j
            gu = gu0 + lu
            np_u = np_us[lu]
            pu = kc // 2
            sl = (2 * gpj) % 4
            v.wait_ge(s_tal, gpj + 1)
            if pu >= np_u - 2:
                ms = (mc % 2) * 2
                if gpj >= 2:
                    v.wait_ge(s_tvp, gpj - 1)
                v.tensor_tensor(msp[:, ms:ms + 2, :],
                                sp[:, sl:sl + 2, :],
                                masks[:, 2 * (pu - (np_u - 2)):2 * (pu - (np_u - 2)) + 2, :],
                                ALU.min).then_inc(s_msp, 1)
                mc += 1
                v.wait_ge(s_msp, mc)
                v.tensor_tensor(pp[:, sl:sl + 2, :],
                                msp[:, ms:ms + 2, :],
                                msp[:, ms:ms + 2, :],
                                ALU.mult).then_inc(s_tvp, 1)
            else:
                v.tensor_tensor(pp[:, sl:sl + 2, :].rearrange("p a b -> p (a b)"),
                                sp[:, sl:sl + 2, :].rearrange("p a b -> p (a b)"),
                                sp[:, sl:sl + 2, :].rearrange("p a b -> p (a b)"),
                                ALU.mult).then_inc(s_tvp, 1)
            if u_end:
                pend_ep.append((j, gu, h2, qb))
            flush_ep(j)
        flush_ep(None)
        return

    def att_sync(sy, grp):
        sy.wait_ge(s_tve, (grp + 1) * AO_UNITS)
        sy.dma_start(out=cc2_in.ap()[grp * 128:(grp + 1) * 128, :],
                     in_=ao[:, :]).then_inc(s_aod, 16)

    with nc.Block() as block:
        @block.gpsimd
        def _(g):
            g.dma_start(out=gw_sb[:, :, :],
                        in_=gwT.rearrange("(cc p) e -> p cc e", p=128)).then_inc(s_gw, 16)
            barrier(0, g, is_dma=True, waits=[(s_gw, 16)])
            ffn_gpsimd(g, 0)
            barrier(1, g, is_dma=True)
            g.wait_ge(s_x1d, 16)
            g.collective_compute("AllGather", ALU.bypass,
                                 replica_groups=[[0, 1], [2, 3], [4, 5], [6, 7]],
                                 ins=[cc1_in.ap()], outs=[cc1_out.ap()]).then_inc(s_cc1, 1)
            att_gpsimd(g, 0)
            barrier(2, g, is_dma=True)
            for grp in range(1, NG):
                att_gpsimd(g, grp)
            barrier(3, g, is_dma=True)
            g.dma_start(out=wp_sb[:, :, :],
                        in_=wpT.rearrange("(cc p) f -> p cc f", p=128)).then_inc(s_pw, 16)
            g.wait_ge(s_aod, NG * 16)
            g.collective_compute("AllGather", ALU.bypass,
                                 replica_groups=[[0, 1], [2, 3], [4, 5], [6, 7]],
                                 ins=[cc2_in.ap()], outs=[cc2_out.ap()]).then_inc(s_cc2, 1)
            ffn_gpsimd(g, 1, lim13=2, lim2=2)
            barrier(4, g, is_dma=True)
            ffn_gpsimd(g, 1, skip13=2, skip2=2)

        @block.sync
        def _(sy):
            sy.dma_start(out=xF[:, :, :],
                         in_=xT.rearrange("(cc p) t -> p cc t", p=128)).then_inc(s_xd, 16)
            sy.dma_start(out=ones_row[:, :], in_=ones_d.ap()).then_inc(s_xd, 16)
            sy.dma_start(out=ones_col[:, :], in_=onesc_d.ap()).then_inc(s_xd, 16)
            sy.dma_start(out=b_sb[:, :],
                         in_=bp.rearrange("(cc p) -> p cc", p=128),
                         allow_slow_non_contiguous=True).then_inc(s_xd, 16)
            sy.dma_start(out=idf_sb[:, :], in_=idf_d.ap()).then_inc(s_xd, 16)
            sy.dma_start(out=selm_sb[:, :], in_=selm_d.ap()).then_inc(s_xd, 16)
            barrier(0, sy, is_dma=True, waits=[(s_xd, 6 * 16)])
            barrier(1, sy, is_dma=True)
            sy.dma_start(out=masks[:, :, :], in_=masks_d.ap()).then_inc(s_xd, 16)
            sy.dma_start(out=idb_sb[:, :], in_=idb_d.ap()).then_inc(s_xd, 16)
            sy.dma_start(out=eps64_sb[:, :], in_=eps64_d.ap()).then_inc(s_xd, 16)
            sy.dma_start(out=ones512[:, :], in_=ones512_d.ap()).then_inc(s_xd, 16)
            sy.wait_ge(s_lnx, 2 * LN_TILES * CC)
            sy.dma_start(out=cc1_in.ap().rearrange("(cc p) t -> p cc t", p=128),
                         in_=xn[:, :, :]).then_inc(s_x1d, 16)
            sy.wait_ge(s_cc1, 1)
            for j in range(2):
                sy.dma_start(out=xn1[:, :, j * T:(j + 1) * T],
                             in_=cc1_out.ap()[j].rearrange("(cc p) t -> p cc t", p=128)
                             ).then_inc(s_x1r, 16)
            barrier(2, sy, is_dma=True, waits=[(s_x1r, 32), (s_xd, 10 * 16)])
            for grp in range(NG):
                att_sync(sy, grp)
            barrier(3, sy, is_dma=True, waits=[(s_aod, NG * 16)])
            sy.wait_ge(s_cc2, 1)
            moff = (sy.partition_id() % 2) * T
            for j in range(2):
                sy.dma_start(out=ao2[:, j * 4:(j + 1) * 4, :],
                             in_=cc2_out.ap()[j].rearrange("(cc p) t -> p cc t", p=128)
                             [:, :, bass.ds(moff, T)]).then_inc(s_ao2, 16)
            barrier(4, sy, is_dma=True, waits=[(s_ao2, 32)])
            for fc in range(CC):
                sy.wait_ge(s_tz, 2 * TZ_UNITS - 16 + 2 * (fc + 1))
                sy.dma_start(out=yT.rearrange("(cc p) t -> p cc t", p=128)[:, fc, :],
                             in_=xF[:, fc, :]).then_inc(s_st, 16)
            sy.wait_ge(s_st, 8 * 16)

        @block.tensor
        def _(t):
            barrier(0, t)
            ln_tensor(t, 0)
            gate_tensor(t, 0)
            ffn_tensor(t, 0)
            ln_tensor(t, 1)
            barrier(1, t)
            barrier(2, t)
            for grp in range(NG):
                att_tensor_kvq(t, grp)
                att_tensor_scores(t, grp)
            barrier(3, t)
            barrier(4, t)
            t.wait_ge(s_ao2, 32)
            t.wait_ge(s_pw, 16)
            pslots = [psA[:, 0:512], psA[:, 512:1024], psC[:, 0:512], psC[:, 512:1024]]
            pj = 0
            for oc in range(CC):
                for tt4 in range(4):
                    if pj >= 4:
                        t.wait_ge(s_prv, pj - 3)
                    for cc in range(CC):
                        ins = t.matmul(pslots[pj % 4], wp_sb[:, cc, oc * 128:(oc + 1) * 128],
                                       ao2[:, cc, tt4 * 512:(tt4 + 1) * 512],
                                       start=(cc == 0), stop=(cc == CC - 1))
                    ins.then_inc(s_pr, 1)
                    pj += 1
            ln_tensor(t, 2)
            gate_tensor(t, 1)
            ffn_tensor(t, 1)

        @block.scalar
        def _(s):
            barrier(0, s)
            ln_scalar(s, 0)
            gate_scalar(s, 0)
            ffn_scalar(s, 0)
            ln_scalar(s, 1)
            barrier(1, s)
            barrier(2, s)
            for grp in range(NG):
                att_scalar_scores(s, grp)
            barrier(3, s)
            barrier(4, s)
            ln_scalar(s, 2)
            gate_scalar(s, 1)
            ffn_scalar(s, 1)

        @block.vector
        def _(v):
            barrier(0, v)
            ln_vector(v, 0)
            gate_vector(v, 0)
            ffn_vector(v, 0)
            ln_vector(v, 1)
            barrier(1, v)
            barrier(2, v)
            for grp in range(NG):
                att_vector_kvq(v, grp)
                att_vector_scores(v, grp)
            barrier(3, v)
            barrier(4, v)
            pslots = [psA[:, 0:512], psA[:, 512:1024], psC[:, 0:512], psC[:, 512:1024]]
            pj = 0
            for oc in range(CC):
                for tt4 in range(4):
                    v.wait_ge(s_pr, NG * AO_UNITS + pj + 1)
                    v.scalar_tensor_tensor(xF[:, oc, tt4 * 512:(tt4 + 1) * 512],
                                           pslots[pj % 4], b_sb[:, oc:oc + 1],
                                           xF[:, oc, tt4 * 512:(tt4 + 1) * 512],
                                           ALU.add, ALU.add).then_inc(s_prv, 1)
                    pj += 1
            v.wait_ge(s_prv, 32)
            ln_vector(v, 2)
            gate_vector(v, 1)
            ffn_vector(v, 1)

    return nc


_NC_CACHE = []


def kernel(L0_attn_x, L0_attn_w_qkv, L0_attn_w_proj, L0_attn_b_proj,
           L0_ffn_gate_w, L0_ffn_w1, L0_ffn_w2, L0_ffn_w3,
           L0_ffn_sw1, L0_ffn_sw2, L0_ffn_sw3):
    x = np.asarray(L0_attn_x)
    w_qkv = np.asarray(L0_attn_w_qkv)
    w_proj = np.asarray(L0_attn_w_proj)
    b_proj = np.ascontiguousarray(np.asarray(L0_attn_b_proj))
    gate_w = np.asarray(L0_ffn_gate_w)
    w1 = np.ascontiguousarray(np.asarray(L0_ffn_w1))
    w2 = np.ascontiguousarray(np.asarray(L0_ffn_w2))
    w3 = np.ascontiguousarray(np.asarray(L0_ffn_w3))
    sw1 = np.ascontiguousarray(np.asarray(L0_ffn_sw1))
    sw2 = np.ascontiguousarray(np.asarray(L0_ffn_sw2))
    sw3 = np.ascontiguousarray(np.asarray(L0_ffn_sw3))

    if not _NC_CACHE:
        _NC_CACHE.append(build())
    nc = _NC_CACHE[0]
    gwT = np.ascontiguousarray(gate_w.T)
    wpT = np.ascontiguousarray(w_proj.T)
    in_maps = []
    for core in range(8):
        bb, m = core // 2, core % 2
        xTc = np.ascontiguousarray(x[bb, m * T:(m + 1) * T, :].T)
        wqTc = np.ascontiguousarray(w_qkv[m * 512:(m + 1) * 512, :].T)
        wkTc = np.ascontiguousarray(w_qkv[D + m * 512:D + (m + 1) * 512, :].T)
        wvTc = np.ascontiguousarray(w_qkv[2 * D + m * 512:2 * D + (m + 1) * 512, :].T)
        in_maps.append(dict(xT=xTc, w1=w1, w2=w2, w3=w3, sw1=sw1, sw2=sw2, sw3=sw3,
                            gwT=gwT, wqT=wqTc, wkT=wkTc, wvT=wvTc, wpT=wpT, bp=b_proj))
    res = run_bass_kernel_spmd(nc, in_maps, core_ids=list(range(8)))
    out = np.empty((B, S, D), dtype=np.float32)
    for core in range(8):
        bb, m = core // 2, core % 2
        out[bb, m * T:(m + 1) * T, :] = res.results[core]["yT"].T
    return out


# version 12
# speedup vs baseline: 1.1482x; 1.0897x over previous
import numpy as np
import ml_dtypes
import concourse.bass as bass
import concourse.mybir as mybir
from concourse.bass_utils import run_bass_kernel_spmd

F32 = mybir.dt.float32
BF16 = mybir.dt.bfloat16
AF = mybir.ActivationFunctionType
ALU = mybir.AluOpType
AX = mybir.AxisListType

B, S, D = 4, 4096, 1024
T = 2048
CC = 8
NE = 8
HC = 22
SHC = 12
NG = 4
NQB = 8
NKC = 32
NTP = 2
NTT = 2
EPS = 1e-5
KB = 1024

EXPS = list(range(NE)) + ["sh"]


def _hc(e):
    return SHC if e == "sh" else HC


AC_UNITS = sum(_hc(e) for e in EXPS) * NTP
Y_UNITS = len(EXPS) * CC * NTP
TZ_UNITS = len(EXPS) * CC * NTP * 2
MOE_Q = NE * CC * NTP * 2
W13_SLABS = sum(_hc(e) // 2 for e in EXPS) * NTP
W2_SLABS = len(EXPS) * CC * NTP
LN_TILES = 4
KCS = [4 * (qb + 1) for qb in range(NQB)]
SC_TILES = 2 * sum(KCS)
SC_PAIRS = SC_TILES // 2
AO_UNITS = 2 * NQB


class C:
    pass


def build():
    nc = bass.Bass("TRN2", target_bir_lowering=False, num_devices=8)

    xT = nc.dram_tensor("xT", [D, T], F32, kind="ExternalInput").ap()
    w1 = nc.dram_tensor("w1", [NE, D, HC * 128], F32, kind="ExternalInput").ap()
    w2 = nc.dram_tensor("w2", [NE, HC * 128, D], F32, kind="ExternalInput").ap()
    w3 = nc.dram_tensor("w3", [NE, D, HC * 128], F32, kind="ExternalInput").ap()
    sw1 = nc.dram_tensor("sw1", [D, SHC * 128], F32, kind="ExternalInput").ap()
    sw2 = nc.dram_tensor("sw2", [SHC * 128, D], F32, kind="ExternalInput").ap()
    sw3 = nc.dram_tensor("sw3", [D, SHC * 128], F32, kind="ExternalInput").ap()
    gwT = nc.dram_tensor("gwT", [D, NE], F32, kind="ExternalInput").ap()
    wqT = nc.dram_tensor("wqT", [D, 512], F32, kind="ExternalInput").ap()
    wkT = nc.dram_tensor("wkT", [D, 512], F32, kind="ExternalInput").ap()
    wvT = nc.dram_tensor("wvT", [D, 512], F32, kind="ExternalInput").ap()
    wpT = nc.dram_tensor("wpT", [D, D], F32, kind="ExternalInput").ap()
    bp = nc.dram_tensor("bp", [D], F32, kind="ExternalInput").ap()
    yT = nc.dram_tensor("yT", [D, T], F32, kind="ExternalOutput").ap()

    cc1_in = nc.dram_tensor("cc1_in", [D, T], BF16)
    cc1_out = nc.dram_tensor("cc1_out", [2, D, T], BF16)
    cc2_in = nc.dram_tensor("cc2_in", [512, S], BF16)
    cc2_out = nc.dram_tensor("cc2_out", [2, 512, S], BF16)

    mask_np = np.zeros((4, 128, 512), dtype=ml_dtypes.bfloat16)
    bigv = 3.3895314e38
    for a in range(4):
        kk = np.arange(128)[:, None]
        qq = np.arange(512)[None, :]
        mask_np[a] = ((kk <= qq - 128 * a) * bigv).astype(ml_dtypes.bfloat16)
    masks_d = nc.inline_tensor(np.ascontiguousarray(mask_np.transpose(1, 0, 2)),
                               name="cmasks")
    ones_d = nc.inline_tensor(np.ones((1, 128), np.float32), name="cones")
    onesc_d = nc.inline_tensor(np.ones((128, 1), dtype=ml_dtypes.bfloat16), name="conesc")
    idf_d = nc.inline_tensor(np.eye(128, dtype=np.float32), name="cidf")
    eps64_np = np.zeros((1, 65), dtype=ml_dtypes.bfloat16); eps64_np[0, 64] = 1.0
    eps64_d = nc.inline_tensor(eps64_np, name="ceps64")
    ones512_d = nc.inline_tensor(np.full((1, 512), 1e-9, dtype=ml_dtypes.bfloat16),
                                 name="cones512")
    selm_np = np.zeros((NE, NE * 128), np.float32)
    for e in range(NE):
        selm_np[e, e * 128:(e + 1) * 128] = 1.0
    selm_d = nc.inline_tensor(selm_np, name="cselm")
    idb_d = nc.inline_tensor(np.eye(128, dtype=ml_dtypes.bfloat16), name="cidb")

    def at(name, shape, dtype, off):
        return nc.alloc_sbuf_tensor_at(name, shape, dtype, offset=off)

    MB = 17920
    xF = at("xF", [128, CC, T], F32, MB)
    A0 = MB + 64 * KB
    xn = at("xn", [128, CC, T], BF16, A0)
    h = at("h", [128, HC, 1024], BF16, A0 + 32 * KB)
    w1s = at("w1s", [128, 2, CC, 2, 128], BF16, A0 + 76 * KB)
    w3s = at("w3s", [128, 2, CC, 2, 128], BF16, A0 + 84 * KB)
    w2s = at("w2s", [128, 2, HC, 128], BF16, A0 + 92 * KB)
    cbt = at("cbt", [128, T], F32, A0 + 105 * KB + 512)
    lg = at("lg", [8, T], F32, A0 + 105 * KB + 512)
    tq = at("tq", [128, 2, 512], F32, A0 + 113 * KB + 512)
    combT = at("combT", [NE, T], F32, A0 + 117 * KB + 512)
    xb = at("xb", [128, 2, 512], BF16, A0 + 125 * KB + 512)
    sqb = at("sqb", [128, 2, 512], BF16, A0 + 127 * KB + 512)
    gt = at("gt", [128, 16, NE], F32, A0 + 129 * KB + 512)
    gs = at("gs", [128, 16, 4], F32, A0 + 130 * KB)
    gmsk = at("gmsk", [128, 16, NE], F32, A0 + 130 * KB + 256)
    selm_sb = at("selm_sb", [NE, NE * 128], F32, A0 + 130 * KB + 768)
    FFN_HIGH = A0 + 134 * KB + 768

    xn1 = at("xn1", [128, CC, S], BF16, A0)
    AT = A0 + 64 * KB
    Kg = at("Kg", [128, S], BF16, AT)
    Vg = at("Vg", [128, NKC, 130], BF16, AT + 8 * KB)
    Qg = at("Qg", [128, S], BF16, AT + 16 * KB + 512)
    Vf = at("Vf", [128, 2, 512], BF16, AT + 24 * KB + 512)
    ao = at("ao", [128, S], BF16, AT + 26 * KB + 512)
    wq_sb = at("wq_sb", [128, CC, 128], BF16, AT + 34 * KB + 512)
    wk_sb = at("wk_sb", [128, CC, 128], BF16, AT + 36 * KB + 512)
    wv_sb = at("wv_sb", [128, CC, 128], BF16, AT + 38 * KB + 512)
    e_scr = at("e_scr", [128, 2, 1024], BF16, AT + 40 * KB + 512)
    eps64_sb = at("eps64_sb", [1, 65], BF16, AT + 44 * KB + 512)
    ones512 = at("ones512", [1, 512], BF16, AT + 44 * KB + 512 + 160)
    pp2 = at("pp2", [128, 2, 512], BF16, AT + 46 * KB)
    sp = at("sp", [128, 4, 512], BF16, AT + 48 * KB + 512)
    pp = at("pp", [128, 4, 512], BF16, AT + 52 * KB + 512)
    msp = at("msp", [128, 4, 512], BF16, AT + 56 * KB + 512)
    rbv = at("rbv", [128, 2, 512], BF16, AT + 60 * KB + 512)
    masks = at("cm_sb", [128, 4, 512], BF16, AT + 62 * KB + 512)
    idb_sb = at("idb_sb", [128, 128], BF16, AT + 66 * KB + 512)
    ATT_HIGH = AT + 66 * KB + 768
    assert ATT_HIGH <= A0 + 130 * KB + 768, (ATT_HIGH, A0)
    ao2 = at("ao2", [128, CC, T], BF16, A0)
    wp_sb = at("wp_sb", [128, CC, 1024], BF16, A0 + 32 * KB)

    ST = FFN_HIGH
    ones_row = at("ones_row", [1, 128], F32, ST)
    ones_col = at("ones_col", [128, 1], BF16, ST + 512)
    b_sb = at("b_sb", [128, CC], F32, ST + 544)
    gw_sb = at("gw_sb", [128, CC, NE], BF16, ST + 576)
    idf_sb = at("idf_sb", [128, 128], F32, ST + 704)
    st_m = at("st_m", [1, 512], F32, ST + 1216)
    st_r = at("st_r", [1, 512], F32, ST + 3264)
    st_t = at("st_t", [1, 512], F32, ST + 5312)
    assert ST + 7360 <= 229376, (ST,)

    psA = nc.alloc_psum_tensor("psA", [128, 1024], F32)
    psC = nc.alloc_psum_tensor("psC", [128, 1024], F32)
    psY = nc.alloc_psum_tensor("psY", [128, 1024], F32)
    psS = nc.alloc_psum_tensor("psS", [128, 512], F32)
    pt = nc.alloc_psum_tensor("pt", [128, 512], BF16)

    sem = nc.alloc_semaphore
    s_xd = sem("s_xd")
    s_gw = sem("s_gw")
    s_w13 = [sem("s_w13_0"), sem("s_w13_1")]
    s_w2 = [sem("s_w2_0"), sem("s_w2_1")]
    s_te, s_ta, s_tv, s_ty, s_tz, s_tq = (sem("s_te"), sem("s_ta"), sem("s_tv"),
                                          sem("s_ty"), sem("s_tz"), sem("s_tq"))
    s_ln, s_lnm, s_lne, s_lnb, s_lnx = (sem("s_ln"), sem("s_lnm"), sem("s_lne"),
                                        sem("s_lnb"), sem("s_lnx"))
    s_lnc, s_lnv, s_lns, s_lnq = sem("s_lnc"), sem("s_lnv"), sem("s_lns"), sem("s_lnq")
    s_lnp = sem("s_lnp")
    s_gl, s_glv, s_gtr, s_gte = sem("s_gl"), sem("s_glv"), sem("s_gtr"), sem("s_gte")
    s_gop, s_gx = sem("s_gop"), sem("s_gx")
    s_cbm, s_cbv = sem("s_cbm"), sem("s_cbv")
    s_cc1, s_cc2 = sem("s_cc1"), sem("s_cc2")
    s_x1d, s_x1r = sem("s_x1d"), sem("s_x1r")
    s_kv, s_kve, s_vt, s_vte, s_aw = (sem("s_kv"), sem("s_kve"), sem("s_vt"),
                                      sem("s_vte"), sem("s_aw"))
    s_tes, s_tae, s_tal, s_tvp, s_msp = (sem("s_tes"), sem("s_tae"), sem("s_tal"),
                                         sem("s_tvp"), sem("s_msp"))
    s_tpv, s_den, s_tve = sem("s_tpv"), sem("s_den"), sem("s_tve")
    s_rcl = sem("s_rcl")
    s_pr, s_prv, s_aod, s_pw, s_ao2 = (sem("s_pr"), sem("s_prv"), sem("s_aod"),
                                       sem("s_pw"), sem("s_ao2"))
    s_st = sem("s_st")
    s_ph = [sem(f"s_ph{i}") for i in range(8)]

    slab13_consumed = {}
    slab2_consumed = {}
    _te = 0
    _ty = 0
    _i13 = 0
    _i2 = 0
    for call in range(2):
        for e in EXPS:
            hcn = _hc(e)
            for tp in range(NTP):
                for hcp in range(hcn // 2):
                    slab13_consumed[_i13] = _te + (2 * hcp + 2)
                    _i13 += 1
                _te += hcn
                for fc in range(CC):
                    slab2_consumed[_i2] = _ty + 1
                    _ty += 1
                    _i2 += 1

    def fb(call):
        return dict(te=call * AC_UNITS, ty=call * Y_UNITS, tz=call * TZ_UNITS,
                    tq=call * MOE_Q, i13=call * W13_SLABS, i2=call * W2_SLABS,
                    cbm=call * NE * 4)

    def lnb(k):
        return dict(ln=k * LN_TILES * CC, lnm=k * LN_TILES, lne=k * LN_TILES,
                    lnb=k * LN_TILES, lnx=k * LN_TILES * CC, lnc=k * LN_TILES * CC,
                    lnv=k * LN_TILES, lns=k * LN_TILES, lnq=k * LN_TILES * CC,
                    lnp=k * LN_TILES * CC)

    def gb(call):
        return dict(gl=call * 4, glv=call * 4, gtr=call * 32, gte=call * 32,
                    gop=call * 11 * 16, gx=call * 16)

    def barrier(i, eng, is_dma=False, waits=()):
        for (ss, vv) in waits:
            eng.wait_ge(ss, vv)
        if not is_dma:
            eng.drain()
        eng.sem_inc(s_ph[i], 1)
        eng.wait_ge(s_ph[i], 5)

    def ln_tensor(t, k):
        b = lnb(k)
        if k == 1:
            t.wait_ge(s_tz, TZ_UNITS)
            t.wait_ge(s_cbv, 32)
        for i in range(LN_TILES):
            for cc in range(CC):
                t.wait_ge(s_ln, b["ln"] + i * CC + cc + 1)
                t.matmul(psS[0:1, :], ones_col[:, 0:1], xb[:, cc % 2, :],
                         start=(cc == 0), stop=(cc == CC - 1))
                t.matmul(psY[0:1, 0:512], ones_col[:, 0:1], sqb[:, cc % 2, :],
                         start=(cc == 0), stop=(cc == CC - 1)).then_inc(s_lnp, 1)
            t.wait_ge(s_lne, b["lne"] + i + 1)
            t.matmul(psA[:, 0:512], ones_row[0:1, :], st_r[0:1, :], start=True, stop=True)
            t.matmul(psC[:, 0:512], ones_row[0:1, :], st_m[0:1, :],
                     start=True, stop=True).then_inc(s_lnb, 1)
            if i < LN_TILES - 1:
                t.wait_ge(s_lnx, b["lnx"] + (i + 1) * CC)

    def ln_scalar(s, k):
        b = lnb(k)
        for i in range(LN_TILES):
            s.wait_ge(s_lnv, b["lnv"] + i + 1)
            s.activation(st_r[0:1, :], st_t[0:1, :], AF.Sqrt).then_inc(s_lns, 1)

    def ln_vector(v, k):
        b = lnb(k)
        for i in range(LN_TILES):
            tok = i * 512
            for cc in range(CC + 1):
                if cc < CC:
                    idx = i * CC + cc
                    if idx >= 2:
                        v.wait_ge(s_lnp, b["lnp"] + idx - 1)
                    v.tensor_copy(xb[:, cc % 2, :], xF[:, cc, tok:tok + 512]).then_inc(s_lnc, 1)
                if cc >= 1:
                    ccp = cc - 1
                    v.wait_ge(s_lnc, b["lnc"] + i * CC + ccp + 1)
                    v.tensor_tensor(sqb[:, ccp % 2, :], xb[:, ccp % 2, :],
                                    xb[:, ccp % 2, :], ALU.mult).then_inc(s_ln, 1)
            v.wait_ge(s_lnp, b["lnp"] + (i + 1) * CC)
            v.wait_ge(s_lns, b["lns"] + i)
            v.tensor_scalar(st_m[0:1, :], psS[0:1, :], 1.0 / D, None, ALU.mult)
            v.tensor_scalar(st_t[0:1, :], psY[0:1, 0:512], 1.0 / D, None, ALU.mult)
            v.drain()
            v.tensor_tensor(st_r[0:1, :], st_m[0:1, :], st_m[0:1, :], ALU.mult)
            v.drain()
            v.tensor_tensor(st_t[0:1, :], st_t[0:1, :], st_r[0:1, :], ALU.subtract)
            v.drain()
            v.tensor_scalar(st_t[0:1, :], st_t[0:1, :], EPS, None,
                            ALU.add).then_inc(s_lnv, 1)
            v.wait_ge(s_lns, b["lns"] + i + 1)
            v.reciprocal(st_r[0:1, :], st_r[0:1, :])
            v.drain()
            v.tensor_tensor(st_m[0:1, :], st_m[0:1, :], st_r[0:1, :],
                            ALU.mult).then_inc(s_lne, 1)
            v.wait_ge(s_lnb, b["lnb"] + i + 1)
            for cc in range(CC + 1):
                if cc < CC:
                    if i * CC + cc >= 2:
                        v.wait_ge(s_lnx, b["lnx"] + i * CC + cc - 1)
                    v.tensor_tensor(tq[:, cc % 2, :], xF[:, cc, tok:tok + 512],
                                    psA[:, 0:512], ALU.mult).then_inc(s_lnq, 1)
                if cc >= 1:
                    ccp = cc - 1
                    v.wait_ge(s_lnq, b["lnq"] + i * CC + ccp + 1)
                    v.tensor_tensor(xn[:, ccp, tok:tok + 512], tq[:, ccp % 2, :],
                                    psC[:, 0:512], ALU.subtract).then_inc(s_lnx, 1)

    def gate_tensor(t, call):
        b = gb(call)
        bl = lnb(2 * call)
        t.wait_ge(s_lnx, bl["lnx"] + LN_TILES * CC)
        for i in range(4):
            if i > 0:
                t.wait_ge(s_glv, b["glv"] + i)
            for cc in range(CC):
                ins = t.matmul(psS[0:NE, :], gw_sb[:, cc, :],
                               xn[:, cc, i * 512:(i + 1) * 512],
                               start=(cc == 0), stop=(cc == CC - 1))
            ins.then_inc(s_gl, 1)
        t.wait_ge(s_glv, b["glv"] + 4)
        for j in range(16):
            if j > 0:
                t.wait_ge(s_gte, b["gte"] + j)
            t.transpose(psS[:, 0:NE], lg[0:NE, j * 128:(j + 1) * 128],
                        idf_sb[0:NE, 0:NE]).then_inc(s_gtr, 1)
        t.wait_ge(s_gop, b["gop"] + 11 * 16)
        for j in range(16):
            if j > 0:
                t.wait_ge(s_gte, b["gte"] + 16 + j)
            t.transpose(psS[0:NE, 0:128], gt[:, j, :],
                        idf_sb[:, :]).then_inc(s_gtr, 1)

    def gate_scalar(s, call):
        b = gb(call)
        for j in range(16):
            s.wait_ge(s_gop, b["gop"] + j + 1)
            s.activation(gt[:, j, :], gt[:, j, :], AF.Exp,
                         bias=gs[:, j, 0:1]).then_inc(s_gx, 1)

    def gate_vector(v, call):
        b = gb(call)
        for i in range(4):
            v.wait_ge(s_gl, b["gl"] + i + 1)
            v.tensor_copy(lg[0:NE, i * 512:(i + 1) * 512], psS[0:NE, :]).then_inc(s_glv, 1)
        for j in range(16):
            v.wait_ge(s_gtr, b["gtr"] + j + 1)
            v.tensor_copy(gt[:, j, :], psS[:, 0:NE]).then_inc(s_gte, 1)
        gop = b["gop"]

        for j in range(16):
            v.wait_ge(s_gte, b["gte"] + j + 1)
            v.reduce_max(gs[:, j, 0:1], gt[:, j, :], axis=AX.X,
                         negate=True).then_inc(s_gop, 1)
        gop += 16
        for j in range(16):
            v.wait_ge(s_gx, b["gx"] + j + 1)
            v.reduce_sum(gs[:, j, 1:2], gt[:, j, :], axis=AX.X).then_inc(s_gop, 1)
        gop += 16

        def step(fn):
            nonlocal gop
            for j in range(16):
                v.wait_ge(s_gop, gop - 16 + j + 1)
                fn(j).then_inc(s_gop, 1)
            gop += 16

        step(lambda j: v.reciprocal(gs[:, j, 2:3], gs[:, j, 1:2]))
        step(lambda j: v.tensor_scalar(gt[:, j, :], gt[:, j, :], gs[:, j, 2:3],
                                       None, ALU.mult))
        step(lambda j: v.reduce_max(gs[:, j, 3:4], gt[:, j, :], axis=AX.X))
        step(lambda j: v.tensor_scalar(gmsk[:, j, :], gt[:, j, :], gs[:, j, 3:4],
                                       None, ALU.is_lt))
        step(lambda j: v.tensor_tensor(gmsk[:, j, :], gmsk[:, j, :], gt[:, j, :],
                                       ALU.mult))
        step(lambda j: v.reduce_max(gs[:, j, 3:4], gmsk[:, j, :], axis=AX.X))
        step(lambda j: v.tensor_scalar(gmsk[:, j, :], gt[:, j, :], gs[:, j, 3:4],
                                       None, ALU.is_ge))
        step(lambda j: v.tensor_tensor(gt[:, j, :], gt[:, j, :], gmsk[:, j, :],
                                       ALU.mult))
        step(lambda j: v.tensor_scalar(gt[:, j, :], gt[:, j, :], 0.5, None, ALU.mult))
        for j in range(16):
            v.wait_ge(s_gtr, b["gtr"] + 16 + j + 1)
            v.tensor_copy(combT[0:NE, j * 128:(j + 1) * 128],
                          psS[0:NE, 0:128]).then_inc(s_gte, 1)

    def ffn_gpsimd(g, call, lim13=None, lim2=None, skip13=0, skip2=0):
        b = fb(call)
        i13 = b["i13"]
        i2 = b["i2"]
        n13 = 0
        n2 = 0
        for e in EXPS:
            hcn = _hc(e)
            for tp in range(NTP):
                for hcp in range(hcn // 2):
                    n13 += 1
                    if (lim13 is not None and n13 > lim13) or n13 <= skip13:
                        i13 += 1
                        continue
                    buf = i13 % 2
                    if i13 >= 2:
                        g.wait_ge(s_te, slab13_consumed[i13 - 2])
                    if e == "sh":
                        sA = sw1.rearrange("(cc p) hh -> p cc hh", p=128)
                        sB = sw3.rearrange("(cc p) hh -> p cc hh", p=128)
                    else:
                        sA = w1[e].rearrange("(cc p) hh -> p cc hh", p=128)
                        sB = w3[e].rearrange("(cc p) hh -> p cc hh", p=128)
                    g.dma_start(out=w1s[:, buf, :, :, :].rearrange("p cc x w -> p cc (x w)"),
                                in_=sA[:, :, hcp * 256:(hcp + 1) * 256]).then_inc(s_w13[buf], 16)
                    g.dma_start(out=w3s[:, buf, :, :, :].rearrange("p cc x w -> p cc (x w)"),
                                in_=sB[:, :, hcp * 256:(hcp + 1) * 256]).then_inc(s_w13[buf], 16)
                    i13 += 1
                for fc in range(CC):
                    n2 += 1
                    if (lim2 is not None and n2 > lim2) or n2 <= skip2:
                        i2 += 1
                        continue
                    buf = i2 % 2
                    if i2 >= 2:
                        g.wait_ge(s_ty, slab2_consumed[i2 - 2])
                    if e == "sh":
                        src2 = sw2.rearrange("(hc p) f -> p hc f", p=128)[
                            :, :, fc * 128:(fc + 1) * 128]
                        dst2 = w2s[:, buf, 0:SHC, :]
                    else:
                        src2 = w2[e].rearrange("(hc p) f -> p hc f", p=128)[
                            :, :, fc * 128:(fc + 1) * 128]
                        dst2 = w2s[:, buf, :, :]
                    g.dma_start(out=dst2, in_=src2).then_inc(s_w2[buf], 16)
                    i2 += 1

    def ffn_tensor(t, call):
        b = fb(call)
        uid = b["te"]
        yid = b["ty"]
        tzc = b["tz"]
        i13 = b["i13"]
        i2 = b["i2"]
        cbm = b["cbm"]
        for ei, e in enumerate(EXPS):
            hcn = _hc(e)
            for tp in range(NTP):
                tok = tp * 1024
                t.wait_ge(s_tz, tzc)
                for hc in range(hcn):
                    hcp, hcx = divmod(hc, 2)
                    slab = i13 + hcp
                    buf = slab % 2
                    if hcx == 0:
                        t.wait_ge(s_w13[buf], (slab // 2 + 1) * 32)
                    ab = psA if uid % 2 == 0 else psY
                    if uid >= 2:
                        t.wait_ge(s_ta, uid - 1)
                    for cc in range(CC):
                        t.matmul(ab[:, 0:512], w1s[:, buf, cc, hcx, :],
                                 xn[:, cc, tok:tok + 512],
                                 start=(cc == 0), stop=(cc == CC - 1))
                        t.matmul(ab[:, 512:1024], w1s[:, buf, cc, hcx, :],
                                 xn[:, cc, tok + 512:tok + 1024],
                                 start=(cc == 0), stop=(cc == CC - 1))
                    if uid >= 1:
                        t.wait_ge(s_tv, uid)
                    for cc in range(CC):
                        t.matmul(psC[:, 0:512], w3s[:, buf, cc, hcx, :],
                                 xn[:, cc, tok:tok + 512],
                                 start=(cc == 0), stop=(cc == CC - 1))
                        ins = t.matmul(psC[:, 512:1024], w3s[:, buf, cc, hcx, :],
                                       xn[:, cc, tok + 512:tok + 1024],
                                       start=(cc == 0), stop=(cc == CC - 1))
                    ins.then_inc(s_te, 1)
                    uid += 1
                i13 += hcn // 2
                if tp == 0 and e != "sh":
                    t.wait_ge(s_gte, gb(call)["gte"] + 32)
                    for j in range(4):
                        if cbm > 0:
                            t.wait_ge(s_cbv, cbm)
                        t.matmul(psS[:, :], selm_sb[:, e * 128:(e + 1) * 128],
                                 combT[0:NE, j * 512:(j + 1) * 512],
                                 start=True, stop=True).then_inc(s_cbm, 1)
                        cbm += 1
                t.wait_ge(s_tv, uid)
                for fc in range(CC):
                    slab2 = i2 + fc
                    buf = slab2 % 2
                    t.wait_ge(s_w2[buf], (slab2 // 2 + 1) * 16)
                    yb = psY if (yid % 2 == 0) else psA
                    if yid >= 2:
                        t.wait_ge(s_tz, tzc - 2)
                    for hc in range(hcn):
                        t.matmul(yb[:, 0:512], w2s[:, buf, hc, :],
                                 h[:, hc, 0:512],
                                 start=(hc == 0), stop=(hc == hcn - 1))
                        ins = t.matmul(yb[:, 512:1024], w2s[:, buf, hc, :],
                                       h[:, hc, 512:1024],
                                       start=(hc == 0), stop=(hc == hcn - 1))
                    ins.then_inc(s_ty, 1)
                    yid += 1
                    tzc += 2
                i2 += CC

    def ffn_scalar(s, call):
        b = fb(call)
        uid = b["te"]
        yid = b["ty"]
        for ei, e in enumerate(EXPS):
            hcn = _hc(e)
            for tp in range(NTP):
                if ei > 0 or tp > 0:
                    s.wait_ge(s_ty, yid)
                yid += CC
                for hc in range(hcn):
                    s.wait_ge(s_te, uid + 1)
                    ab = psA if uid % 2 == 0 else psY
                    s.activation(h[:, hc, :], ab[:, :], AF.Silu).then_inc(s_ta, 1)
                    uid += 1

    def ffn_vector(v, call):
        b = fb(call)
        uid = b["te"]
        yid = b["ty"]
        qid = b["tq"]
        tzc = b["tz"]
        cbv = b["cbm"]
        pend = None
        tz_of_add = {}

        def flush():
            nonlocal pend, tzc
            if pend is not None:
                v.wait_ge(s_tq, pend[0] + 1)
                v.tensor_tensor(pend[1], pend[1], tq[:, pend[0] % 2, :],
                                ALU.add).then_inc(s_tz, 1)
                tzc += 1
                tz_of_add[pend[0]] = tzc
                pend = None

        for ei, e in enumerate(EXPS):
            hcn = _hc(e)
            for tp in range(NTP):
                tok = tp * 1024
                flush()
                if ei > 0 or tp > 0:
                    v.wait_ge(s_ty, yid)
                for hc in range(hcn):
                    v.wait_ge(s_ta, uid + 1)
                    v.tensor_tensor(h[:, hc, :], h[:, hc, :], psC[:, :],
                                    ALU.mult).then_inc(s_tv, 1)
                    uid += 1
                if tp == 0 and e != "sh":
                    flush()
                    v.wait_ge(s_tz, tzc)
                    for j in range(4):
                        v.wait_ge(s_cbm, cbv + j + 1)
                        v.tensor_copy(cbt[:, j * 512:(j + 1) * 512], psS[:, :]).then_inc(s_cbv, 1)
                    cbv += 4
                for fc in range(CC):
                    yb = psY if (yid % 2 == 0) else psA
                    v.wait_ge(s_ty, yid + 1)
                    for half in range(2):
                        tk = tok + half * 512
                        if e == "sh":
                            flush()
                            v.scalar_tensor_tensor(xF[:, fc, tk:tk + 512],
                                                   yb[:, half * 512:(half + 1) * 512], 0.5,
                                                   xF[:, fc, tk:tk + 512],
                                                   ALU.mult, ALU.add).then_inc(s_tz, 1)
                            tzc += 1
                        else:
                            if qid >= 2 and (qid - 2) in tz_of_add:
                                v.wait_ge(s_tz, tz_of_add[qid - 2])
                            v.tensor_tensor(tq[:, qid % 2, :],
                                            yb[:, half * 512:(half + 1) * 512],
                                            cbt[:, tk:tk + 512], ALU.mult).then_inc(s_tq, 1)
                            flush()
                            pend = (qid, xF[:, fc, tk:tk + 512])
                            qid += 1
                    yid += 1
        flush()

    def att_gpsimd(g, grp):
        if grp >= 1:
            g.wait_ge(s_kv, 24 * grp)
        for w_in, w_out in ((wqT, wq_sb), (wkT, wk_sb), (wvT, wv_sb)):
            g.dma_start(out=w_out[:, :, :],
                        in_=w_in.rearrange("(cc p) f -> p cc f", p=128)[
                            :, :, grp * 128:(grp + 1) * 128]).then_inc(s_aw, 16)

    def att_tensor_kvq(t, grp):
        t.wait_ge(s_aw, (grp + 1) * 48)
        if grp >= 1:
            t.wait_ge(s_tae, grp * SC_PAIRS)
        slots = [psA[:, 0:512], psA[:, 512:1024], psC[:, 0:512], psC[:, 512:1024]]
        kv = grp * 24
        vt = grp * 32
        for tt in range(8):
            if kv >= 4:
                t.wait_ge(s_kve, kv - 3)
            for cc in range(CC):
                ins = t.matmul(slots[kv % 4], wk_sb[:, cc, :],
                               xn1[:, cc, tt * 512:(tt + 1) * 512],
                               start=(cc == 0), stop=(cc == CC - 1))
            ins.then_inc(s_kv, 1)
            kv += 1
        for tt in range(8):
            if kv >= 4:
                t.wait_ge(s_kve, kv - 3)
            for cc in range(CC):
                ins = t.matmul(slots[kv % 4], wv_sb[:, cc, :],
                               xn1[:, cc, tt * 512:(tt + 1) * 512],
                               start=(cc == 0), stop=(cc == CC - 1))
            ins.then_inc(s_kv, 1)
            kv += 1
            for j in range(4):
                t.wait_ge(s_kve, kv)
                if vt >= 1:
                    t.wait_ge(s_vte, vt)
                t.transpose(pt[:, 0:128],
                            Vf[:, tt % 2, j * 128:(j + 1) * 128],
                            idb_sb[:, :]).then_inc(s_vt, 1)
                vt += 1
        for tt in range(8):
            if kv >= 4:
                t.wait_ge(s_kve, kv - 3)
            for cc in range(CC):
                ins = t.matmul(slots[kv % 4], wq_sb[:, cc, :],
                               xn1[:, cc, tt * 512:(tt + 1) * 512],
                               start=(cc == 0), stop=(cc == CC - 1))
            ins.then_inc(s_kv, 1)
            kv += 1

    def att_vector_kvq(v, grp):
        slots = [psA[:, 0:512], psA[:, 512:1024], psC[:, 0:512], psC[:, 512:1024]]
        kv = grp * 24
        vt = grp * 32
        if grp >= 1:
            v.wait_ge(s_tpv, grp * AO_UNITS)
        v.memset(Vg.ap().rearrange("p k (x f) -> p k x f", x=2)[:, :, :, 64:65], 1.0)
        for tt in range(8):
            v.wait_ge(s_kv, kv + 1)
            v.tensor_copy(Kg[:, tt * 512:(tt + 1) * 512], slots[kv % 4]).then_inc(s_kve, 1)
            kv += 1
        for tt in range(8):
            v.wait_ge(s_kv, kv + 1)
            v.tensor_copy(Vf[:, tt % 2, :], slots[kv % 4]).then_inc(s_kve, 1)
            kv += 1
            for j in range(4):
                kc = tt * 4 + j
                v.wait_ge(s_vt, vt + 1)
                v.tensor_copy(
                    Vg[:, kc, :].rearrange("p (x f) -> p x f", x=2)[:, :, 0:64],
                    pt[:, 0:128].rearrange("p (x f) -> p x f", x=2)[:, :, 0:64],
                ).then_inc(s_vte, 1)
                vt += 1
        for tt in range(8):
            v.wait_ge(s_kv, kv + 1)
            v.tensor_copy(Qg[:, tt * 512:(tt + 1) * 512], slots[kv % 4]).then_inc(s_kve, 1)
            kv += 1

    def att_pairs():
        pairs = []
        for qb in range(NQB):
            L = KCS[qb]
            for pu in range(L // 2):
                for h2 in range(2):
                    pairs.append((h2, qb, 2 * pu, pu == 0, pu == L // 2 - 1,
                                  2 * qb + h2))
        return pairs

    def att_tensor_scores(t, grp):
        pj0 = grp * SC_PAIRS
        gu0 = grp * AO_UNITS
        t.wait_ge(s_kve, (grp + 1) * 24)
        t.wait_ge(s_vte, (grp + 1) * 32)
        pairs = att_pairs()
        pbank = [psA, psC]
        pend_den = []

        def flush_den(now_j):
            while pend_den and (now_j is None or pend_den[0][0] <= now_j - 2):
                _, gu = pend_den.pop(0)
                t.wait_ge(s_den, gu + 1)
                if gu >= 1:
                    t.wait_ge(s_tve, gu)
                t.matmul(psS[0:64, :], ones_row[0:1, 0:64], st_r[0:1, :],
                         start=True, stop=True).then_inc(s_pr, 1)

        def emit_score_pair(j):
            h2, qb, kc, _, _, _ = pairs[j]
            gpj = pj0 + j
            if gpj >= 2:
                t.wait_ge(s_tae, gpj - 1)
            bk = pbank[gpj % 2]
            for half in range(2):
                ins = t.matmul(bk[:, half * 512:(half + 1) * 512],
                               Kg[h2 * 64:(h2 + 1) * 64,
                                  (kc + half) * 128:(kc + half + 1) * 128],
                               Qg[h2 * 64:(h2 + 1) * 64, qb * 512:(qb + 1) * 512],
                               start=True, stop=True)
            ins.then_inc(s_tes, 1)

        emit_score_pair(0)
        if len(pairs) > 1:
            emit_score_pair(1)
        for j, (h2, qb, kc, u_start, u_end, lu) in enumerate(pairs):
            gpj = pj0 + j
            gu = gu0 + lu
            yh = psY[:, h2 * 512:(h2 + 1) * 512]
            if j + 2 < len(pairs):
                emit_score_pair(j + 2)
            flush_den(j)
            if u_start:
                if gu >= 2:
                    t.wait_ge(s_tve, gu - 1)
                t.matmul(yh[0:65, :], eps64_sb[0:1, :], ones512[0:1, :],
                         start=True, stop=False)
            t.wait_ge(s_tvp, gpj + 1)
            for half in range(2):
                k6 = (2 * gpj + half) % 6
                ins = t.matmul(yh[0:65, :],
                               Vg[:, kc + half, h2 * 65:(h2 + 1) * 65],
                               pp[:, k6, :] if k6 < 4 else pp2[:, k6 - 4, :],
                               start=False, stop=(u_end and half == 1))
            if u_end:
                ins.then_inc(s_tpv, 1)
                pend_den.append((j, gu))
        flush_den(None)
        return

    def att_scalar_scores(s, grp):
        pj0 = grp * SC_PAIRS
        gu0 = grp * AO_UNITS
        pbank = [psA, psC]
        pairs = att_pairs()
        pend_rc = []

        def flush_rc(now_j):
            while pend_rc and (now_j is None or pend_rc[0][0] <= now_j - 2):
                _, gu, o2 = pend_rc.pop(0)
                if gu >= 1:
                    s.wait_ge(s_den, gu)
                s.wait_ge(s_tpv, gu + 1)
                s.activation(st_t[0:1, :], psY[64:65, o2 * 512:(o2 + 1) * 512],
                             AF.Ln).then_inc(s_rcl, 1)
                if gu >= 1:
                    s.wait_ge(s_pr, gu)
                s.wait_ge(s_rcl, gu + 1)
                s.activation(st_r[0:1, :], st_t[0:1, :], AF.Exp,
                             scale=-1.0).then_inc(s_den, 1)

        for j, (h2, qb, kc, u_start, u_end, lu) in enumerate(pairs):
            gpj = pj0 + j
            s.wait_ge(s_tes, gpj + 1)
            if gpj >= 2:
                s.wait_ge(s_tal, gpj - 1)
            s.activation(e_scr[:, gpj % 2, :], pbank[gpj % 2][:, :], AF.Exp,
                         scale=0.125).then_inc(s_tae, 1)
            s.wait_ge(s_tae, gpj + 1)
            if gpj >= 2:
                s.wait_ge(s_tvp, gpj - 1)
            s.activation(sp[:, (2 * gpj) % 4:(2 * gpj) % 4 + 2, :]
                         .rearrange("p a b -> p (a b)"),
                         e_scr[:, gpj % 2, :],
                         AF.Ln, bias=1.0).then_inc(s_tal, 1)
            if u_end:
                pend_rc.append((j, gu0 + lu, h2))
            flush_rc(j)
        flush_rc(None)

    def att_vector_scores(v, grp):
        pj0 = grp * SC_PAIRS
        gu0 = grp * AO_UNITS
        mc = grp * AO_UNITS * 2
        if grp >= 1:
            v.wait_ge(s_aod, grp * 16)
        pairs = att_pairs()
        np_us = {}
        for h2, qb, kc, u_start, u_end, lu in pairs:
            np_us[lu] = KCS[qb] // 2
        pend_ep = []

        def flush_ep(now_j):
            while pend_ep and (now_j is None or pend_ep[0][0] <= now_j - 2):
                _, gu, h2, qb = pend_ep.pop(0)
                o2 = h2
                v.wait_ge(s_pr, gu + 1)
                if gu >= 2:
                    v.wait_ge(s_tve, gu - 1)
                v.tensor_copy(rbv[0:64, o2, :], psS[0:64, :])
                v.drain()
                v.tensor_tensor(ao[h2 * 64:(h2 + 1) * 64, qb * 512:(qb + 1) * 512],
                                psY[0:64, o2 * 512:(o2 + 1) * 512], rbv[0:64, o2, :],
                                ALU.mult).then_inc(s_tve, 1)

        mspc = {}
        for j, (h2, qb, kc, u_start, u_end, lu) in enumerate(pairs):
            gpj = pj0 + j
            gu = gu0 + lu
            np_u = np_us[lu]
            pu = kc // 2
            sl = (2 * gpj) % 4
            slp = (2 * gpj) % 6
            ppw = pp[:, slp:slp + 2, :] if slp < 4 else pp2[:, 0:2, :]
            v.wait_ge(s_tal, gpj + 1)
            if pu >= np_u - 2:
                ms = (mc % 2) * 2
                if gpj >= 2:
                    v.wait_ge(s_tvp, gpj - 1)
                v.tensor_tensor(msp[:, ms:ms + 2, :],
                                sp[:, sl:sl + 2, :],
                                masks[:, 2 * (pu - (np_u - 2)):2 * (pu - (np_u - 2)) + 2, :],
                                ALU.min).then_inc(s_msp, 1)
                mc += 1
                v.wait_ge(s_msp, mc)
                v.tensor_tensor(ppw,
                                msp[:, ms:ms + 2, :],
                                msp[:, ms:ms + 2, :],
                                ALU.mult).then_inc(s_tvp, 1)
            else:
                v.tensor_tensor(ppw.rearrange("p a b -> p (a b)"),
                                sp[:, sl:sl + 2, :].rearrange("p a b -> p (a b)"),
                                sp[:, sl:sl + 2, :].rearrange("p a b -> p (a b)"),
                                ALU.mult).then_inc(s_tvp, 1)
            if u_end:
                pend_ep.append((j, gu, h2, qb))
            flush_ep(j)
        flush_ep(None)
        return

    def att_sync(sy, grp):
        sy.wait_ge(s_tve, (grp + 1) * AO_UNITS)
        sy.dma_start(out=cc2_in.ap()[grp * 128:(grp + 1) * 128, :],
                     in_=ao[:, :]).then_inc(s_aod, 16)

    with nc.Block() as block:
        @block.gpsimd
        def _(g):
            g.dma_start(out=gw_sb[:, :, :],
                        in_=gwT.rearrange("(cc p) e -> p cc e", p=128)).then_inc(s_gw, 16)
            barrier(0, g, is_dma=True, waits=[(s_gw, 16)])
            ffn_gpsimd(g, 0)
            barrier(1, g, is_dma=True)
            g.wait_ge(s_x1d, 16)
            g.collective_compute("AllGather", ALU.bypass,
                                 replica_groups=[[0, 1], [2, 3], [4, 5], [6, 7]],
                                 ins=[cc1_in.ap()], outs=[cc1_out.ap()]).then_inc(s_cc1, 1)
            att_gpsimd(g, 0)
            barrier(2, g, is_dma=True)
            for grp in range(1, NG):
                att_gpsimd(g, grp)
            barrier(3, g, is_dma=True)
            g.dma_start(out=wp_sb[:, :, :],
                        in_=wpT.rearrange("(cc p) f -> p cc f", p=128)).then_inc(s_pw, 16)
            g.wait_ge(s_aod, NG * 16)
            g.collective_compute("AllGather", ALU.bypass,
                                 replica_groups=[[0, 1], [2, 3], [4, 5], [6, 7]],
                                 ins=[cc2_in.ap()], outs=[cc2_out.ap()]).then_inc(s_cc2, 1)
            ffn_gpsimd(g, 1, lim13=2, lim2=2)
            barrier(4, g, is_dma=True)
            ffn_gpsimd(g, 1, skip13=2, skip2=2)

        @block.sync
        def _(sy):
            sy.dma_start(out=xF[:, :, :],
                         in_=xT.rearrange("(cc p) t -> p cc t", p=128)).then_inc(s_xd, 16)
            sy.dma_start(out=ones_row[:, :], in_=ones_d.ap()).then_inc(s_xd, 16)
            sy.dma_start(out=ones_col[:, :], in_=onesc_d.ap()).then_inc(s_xd, 16)
            sy.dma_start(out=b_sb[:, :],
                         in_=bp.rearrange("(cc p) -> p cc", p=128),
                         allow_slow_non_contiguous=True).then_inc(s_xd, 16)
            sy.dma_start(out=idf_sb[:, :], in_=idf_d.ap()).then_inc(s_xd, 16)
            sy.dma_start(out=selm_sb[:, :], in_=selm_d.ap()).then_inc(s_xd, 16)
            barrier(0, sy, is_dma=True, waits=[(s_xd, 6 * 16)])
            barrier(1, sy, is_dma=True)
            sy.dma_start(out=masks[:, :, :], in_=masks_d.ap()).then_inc(s_xd, 16)
            sy.dma_start(out=idb_sb[:, :], in_=idb_d.ap()).then_inc(s_xd, 16)
            sy.dma_start(out=eps64_sb[:, :], in_=eps64_d.ap()).then_inc(s_xd, 16)
            sy.dma_start(out=ones512[:, :], in_=ones512_d.ap()).then_inc(s_xd, 16)
            sy.wait_ge(s_lnx, 2 * LN_TILES * CC)
            sy.dma_start(out=cc1_in.ap().rearrange("(cc p) t -> p cc t", p=128),
                         in_=xn[:, :, :]).then_inc(s_x1d, 16)
            sy.wait_ge(s_cc1, 1)
            for j in range(2):
                sy.dma_start(out=xn1[:, :, j * T:(j + 1) * T],
                             in_=cc1_out.ap()[j].rearrange("(cc p) t -> p cc t", p=128)
                             ).then_inc(s_x1r, 16)
            barrier(2, sy, is_dma=True, waits=[(s_x1r, 32), (s_xd, 10 * 16)])
            for grp in range(NG):
                att_sync(sy, grp)
            barrier(3, sy, is_dma=True, waits=[(s_aod, NG * 16)])
            sy.wait_ge(s_cc2, 1)
            moff = (sy.partition_id() % 2) * T
            for j in range(2):
                sy.dma_start(out=ao2[:, j * 4:(j + 1) * 4, :],
                             in_=cc2_out.ap()[j].rearrange("(cc p) t -> p cc t", p=128)
                             [:, :, bass.ds(moff, T)]).then_inc(s_ao2, 16)
            barrier(4, sy, is_dma=True, waits=[(s_ao2, 32)])
            for fc in range(CC):
                sy.wait_ge(s_tz, 2 * TZ_UNITS - 16 + 2 * (fc + 1))
                sy.dma_start(out=yT.rearrange("(cc p) t -> p cc t", p=128)[:, fc, :],
                             in_=xF[:, fc, :]).then_inc(s_st, 16)
            sy.wait_ge(s_st, 8 * 16)

        @block.tensor
        def _(t):
            barrier(0, t)
            ln_tensor(t, 0)
            gate_tensor(t, 0)
            ffn_tensor(t, 0)
            ln_tensor(t, 1)
            barrier(1, t)
            barrier(2, t)
            for grp in range(NG):
                att_tensor_kvq(t, grp)
                att_tensor_scores(t, grp)
            barrier(3, t)
            barrier(4, t)
            t.wait_ge(s_ao2, 32)
            t.wait_ge(s_pw, 16)
            pslots = [psA[:, 0:512], psA[:, 512:1024], psC[:, 0:512], psC[:, 512:1024]]
            pj = 0
            for oc in range(CC):
                for tt4 in range(4):
                    if pj >= 4:
                        t.wait_ge(s_prv, pj - 3)
                    for cc in range(CC):
                        ins = t.matmul(pslots[pj % 4], wp_sb[:, cc, oc * 128:(oc + 1) * 128],
                                       ao2[:, cc, tt4 * 512:(tt4 + 1) * 512],
                                       start=(cc == 0), stop=(cc == CC - 1))
                    ins.then_inc(s_pr, 1)
                    pj += 1
            ln_tensor(t, 2)
            gate_tensor(t, 1)
            ffn_tensor(t, 1)

        @block.scalar
        def _(s):
            barrier(0, s)
            ln_scalar(s, 0)
            gate_scalar(s, 0)
            ffn_scalar(s, 0)
            ln_scalar(s, 1)
            barrier(1, s)
            barrier(2, s)
            for grp in range(NG):
                att_scalar_scores(s, grp)
            barrier(3, s)
            barrier(4, s)
            ln_scalar(s, 2)
            gate_scalar(s, 1)
            ffn_scalar(s, 1)

        @block.vector
        def _(v):
            barrier(0, v)
            ln_vector(v, 0)
            gate_vector(v, 0)
            ffn_vector(v, 0)
            ln_vector(v, 1)
            barrier(1, v)
            barrier(2, v)
            for grp in range(NG):
                att_vector_kvq(v, grp)
                att_vector_scores(v, grp)
            barrier(3, v)
            barrier(4, v)
            pslots = [psA[:, 0:512], psA[:, 512:1024], psC[:, 0:512], psC[:, 512:1024]]
            pj = 0
            for oc in range(CC):
                for tt4 in range(4):
                    v.wait_ge(s_pr, NG * AO_UNITS + pj + 1)
                    v.scalar_tensor_tensor(xF[:, oc, tt4 * 512:(tt4 + 1) * 512],
                                           pslots[pj % 4], b_sb[:, oc:oc + 1],
                                           xF[:, oc, tt4 * 512:(tt4 + 1) * 512],
                                           ALU.add, ALU.add).then_inc(s_prv, 1)
                    pj += 1
            v.wait_ge(s_prv, 32)
            ln_vector(v, 2)
            gate_vector(v, 1)
            ffn_vector(v, 1)

    return nc


_NC_CACHE = []


def kernel(L0_attn_x, L0_attn_w_qkv, L0_attn_w_proj, L0_attn_b_proj,
           L0_ffn_gate_w, L0_ffn_w1, L0_ffn_w2, L0_ffn_w3,
           L0_ffn_sw1, L0_ffn_sw2, L0_ffn_sw3):
    x = np.asarray(L0_attn_x)
    w_qkv = np.asarray(L0_attn_w_qkv)
    w_proj = np.asarray(L0_attn_w_proj)
    b_proj = np.ascontiguousarray(np.asarray(L0_attn_b_proj))
    gate_w = np.asarray(L0_ffn_gate_w)
    w1 = np.ascontiguousarray(np.asarray(L0_ffn_w1))
    w2 = np.ascontiguousarray(np.asarray(L0_ffn_w2))
    w3 = np.ascontiguousarray(np.asarray(L0_ffn_w3))
    sw1 = np.ascontiguousarray(np.asarray(L0_ffn_sw1))
    sw2 = np.ascontiguousarray(np.asarray(L0_ffn_sw2))
    sw3 = np.ascontiguousarray(np.asarray(L0_ffn_sw3))

    if not _NC_CACHE:
        _NC_CACHE.append(build())
    nc = _NC_CACHE[0]
    gwT = np.ascontiguousarray(gate_w.T)
    wpT = np.ascontiguousarray(w_proj.T)
    in_maps = []
    for core in range(8):
        bb, m = core // 2, core % 2
        xTc = np.ascontiguousarray(x[bb, m * T:(m + 1) * T, :].T)
        wqTc = np.ascontiguousarray(w_qkv[m * 512:(m + 1) * 512, :].T)
        wkTc = np.ascontiguousarray(w_qkv[D + m * 512:D + (m + 1) * 512, :].T)
        wvTc = np.ascontiguousarray(w_qkv[2 * D + m * 512:2 * D + (m + 1) * 512, :].T)
        in_maps.append(dict(xT=xTc, w1=w1, w2=w2, w3=w3, sw1=sw1, sw2=sw2, sw3=sw3,
                            gwT=gwT, wqT=wqTc, wkT=wkTc, wvT=wvTc, wpT=wpT, bp=b_proj))
    res = run_bass_kernel_spmd(nc, in_maps, core_ids=list(range(8)))
    out = np.empty((B, S, D), dtype=np.float32)
    for core in range(8):
        bb, m = core // 2, core % 2
        out[bb, m * T:(m + 1) * T, :] = res.results[core]["yT"].T
    return out
